# version 19
# baseline (speedup 1.0000x reference)
import numpy as np
from contextlib import ExitStack
import concourse.bass as bass
import concourse.mybir as mybir

F32 = mybir.dt.float32
BF16 = mybir.dt.bfloat16
I32 = mybir.dt.int32
AF = mybir.ActivationFunctionType
ALU = mybir.AluOpType
AX = mybir.AxisListType

ENGINES = ("sync", "tensor", "vector", "scalar", "gpsimd")


class Sched:
    def __init__(self):
        self.q = {e: [] for e in ENGINES}
        self.cnt = {}
        self.sems = {}

    def setup(self, nc, stack, names):
        for n in names:
            self.sems[n] = stack.enter_context(nc.semaphore(n))
            self.cnt[n] = 0

    def run(self, eng, fn, waits=(), inc=None):
        wl = {}
        for t in waits:
            if t is None:
                continue
            ch, val = t
            if val > 0:
                wl[ch] = max(wl.get(ch, 0), val)
        tok = None
        if inc is not None:
            ch, amt = inc
            self.cnt[ch] += amt
            tok = (ch, self.cnt[ch])
        self.q[eng].append((wl, fn, inc))
        return tok

    def emit_into(self, block):
        sched = self
        waited = {}
        for eng in ENGINES:
            for wl, fn, inc in self.q[eng]:
                for ch, val in wl.items():
                    waited.setdefault(ch, set()).add(val)
        keepsets = {}
        remap = {}
        for ch, vals in waited.items():
            sv = sorted(vals)
            keepsets[ch] = set(sv)
            amt = 16 if (ch.startswith("ld") or ch.startswith("sp") or ch.startswith("lc")) else 1
            remap[ch] = {v: (i + 1) * amt for i, v in enumerate(sv)}

        def body_for(name):
            def body(e):
                cnt = {}
                for wl, fn, inc in sched.q[name]:
                    for ch, val in wl.items():
                        e.wait_ge(sched.sems[ch], remap[ch][val])
                    if fn is not None:
                        inst = fn(e)
                        if inc is not None:
                            ch, amt = inc
                            cnt[ch] = cnt.get(ch, 0) + amt
                            if ch in keepsets and cnt[ch] in keepsets[ch]:
                                inst.then_inc(sched.sems[ch], amt)
            return body

        block.sync(body_for("sync"))
        block.tensor(body_for("tensor"))
        block.vector(body_for("vector"))
        block.scalar(body_for("scalar"))
        block.gpsimd(body_for("gpsimd"))


def build_program(cfg):
    D = cfg["D"]; SL = cfg["SL"]; B = cfg["B"]; NCORE = cfg["NCORE"]
    KC = D // 128
    H = D // 64
    NJ = H // 2
    NT = SL // 128
    CPB = NCORE // B
    R = D // 128
    NPW = D * D // 512
    NH = D // 512
    assert D % 512 == 0 and SL % 512 == 0

    nc = bass.Bass("TRN2", target_bir_lowering=False)

    x_in = nc.dram_tensor("x_in", [SL, D], F32, kind="ExternalInput").ap()
    wq_in = nc.dram_tensor("wq", [D, D], F32, kind="ExternalInput").ap()
    wk_in = nc.dram_tensor("wk", [D, D], F32, kind="ExternalInput").ap()
    wv_in = nc.dram_tensor("wv", [D, D], F32, kind="ExternalInput").ap()
    wo_in = nc.dram_tensor("wo", [D, D], F32, kind="ExternalInput").ap()
    pk_in = nc.dram_tensor("pk", [128, R * D], I32, kind="ExternalInput").ap()
    wsc_in = nc.dram_tensor("wsc", [128, 1], F32, kind="ExternalInput").ap()
    bias_in = nc.dram_tensor("bias_rep", [128, D], F32, kind="ExternalInput").ap()
    ident_in = nc.dram_tensor("ident", [128, 128], F32, kind="ExternalInput").ap()
    maskL_in = nc.dram_tensor("maskL", [128, 128], F32, kind="ExternalInput").ap()
    pm_in = nc.dram_tensor("pmask", [128, CPB], F32, kind="ExternalInput").ap()
    out = nc.dram_tensor("out", [SL, D], F32, kind="ExternalOutput").ap()

    qt_h = nc.dram_tensor("qt_h", [128, NJ, SL], F32).ap()
    kt_h = nc.dram_tensor("kt_h", [128, NJ, SL], F32).ap()
    v_h = nc.dram_tensor("v_h", [SL, D], F32).ap()
    g1_h = nc.dram_tensor("g1_h", [NT, 128, NJ * 128], F32).ap()
    g2_h = nc.dram_tensor("g2_h", [NT, 128, D], F32).ap()
    summ_t = nc.dram_tensor("summ", [3 * 128, NJ * 64], F32)
    summ_all_t = nc.dram_tensor("summ_all", [CPB * 3 * 128, NJ * 64], F32)

    S = Sched()
    stack = ExitStack()
    with stack:
        sbuf = lambda name, shape, dt=F32: stack.enter_context(nc.sbuf_tensor(name, shape, dt))

        ident = sbuf("identt", [128, 128])
        maskL = sbuf("maskl", [128, 128])
        pm = sbuf("pms", [128, CPB])
        wscv = sbuf("wscv", [128, 1])
        bias = sbuf("biasr", [128, D])
        qint = sbuf("qint", [128, max(2 * NPW, 2 * D)], I32)
        enc = qint[:, 0:NPW]
        pk = qint[:, NPW:2 * NPW]
        q32b = qint[:, D:2 * D]
        pv = sbuf("pvs", [128, 1])
        p7 = sbuf("p7s", [128, 1])
        p7i = sbuf("p7i", [128, 1], I32)
        mrs = sbuf("mrs", [128, 1])
        halfv = sbuf("halfv", [128, 1])
        tffn = sbuf("tffn", [128, KC * D], BF16)
        qxtb = sbuf("qxtb", [128, D], BF16)
        WW = NJ * 64
        lnxT = sbuf("lnxT", [128, max(KC * SL, 8 * D + 4 * CPB * WW)])
        wun = lnxT[:, 0:R * D]
        wbig = sbuf("wbig", [128, max(KC * D, 2 * NJ * 128 + 4 * D)])
        ktc2 = [wbig[:, 0:NJ * 128], wbig[:, NJ * 128:2 * NJ * 128]]
        ktok2 = [wbig[:, 2 * NJ * 128:2 * NJ * 128 + D],
                 wbig[:, 2 * NJ * 128 + D:2 * NJ * 128 + 2 * D]]
        qtok2 = [wbig[:, 2 * NJ * 128 + 2 * D:2 * NJ * 128 + 3 * D],
                 wbig[:, 2 * NJ * 128 + 3 * D:2 * NJ * 128 + 4 * D]]
        xt = None
        stgbig = sbuf("stgbig", [128, 1024])
        stg = [stgbig[:, 0:512], stgbig[:, 512:1024]]
        qtc2 = [sbuf("qtc0", [128, NJ * 128]), sbuf("qtc1", [128, NJ * 128])]
        vc2 = [sbuf("vc0", [128, D]), sbuf("vc1", [128, D])]
        g1c2 = [sbuf("g1c0", [128, NJ * 128]), sbuf("g1c1", [128, NJ * 128])]
        g2c2 = [sbuf("g2c0", [128, D]), sbuf("g2c1", [128, D])]
        otc3 = [sbuf("otc0", [128, NJ * 128]), sbuf("otc1", [128, NJ * 128]),
                sbuf("otc2x", [128, NJ * 128])]
        asm = [sbuf("asm0", [128, 128]), sbuf("asm1", [128, 128]),
               sbuf("asm2", [128, 128]), sbuf("asm3", [128, 128])]
        g1f2 = [sbuf("g1f0", [128, 128]), sbuf("g1f1", [128, 128])]
        g2f2 = [sbuf("g2f0", [128, 128]), sbuf("g2f1", [128, 128])]
        s1run = sbuf("s1run", [128, NJ * 128])
        hloc = sbuf("hloc", [128, NJ * 64])
        crun = sbuf("crun", [128, NJ * 64])
        s1in = sbuf("s1in", [128, NJ * 64])
        hin = sbuf("hin", [128, NJ * 64])
        hbd = sbuf("hbd", [128, NJ * 128])
        s1bd = sbuf("s1bd", [128, NJ * 128])
        pbd = otc3[0][:, 0:NJ * 128]
        tmpst = otc3[1][:, 0:NJ * 64]
        nmv = sbuf("nmv", [128, 1])
        vsv = sbuf("vsv", [128, 1])
        rrv = sbuf("rrv", [128, 1])
        rsv = sbuf("rsv", [128, 1])
        nwa = sbuf("nwa", [128, 1])
        sqd = qtc2[0][:, 0:min(D, 1024)]
        sqd5 = stgbig[:, 0:min(D, 1024)]
        xt = [otc3[0][:, 0:D], otc3[2][:, 0:D], vc2[0][:, 0:D]]
        mxv = sbuf("mxv", [128, 1])
        scv = sbuf("scv", [128, 1])
        epv = sbuf("epv", [128, 1])
        q32 = qint[:, 0:D]
        sa_a = lnxT[:, 7 * D + 0 * CPB * WW: 7 * D + 1 * CPB * WW]
        sa_b = lnxT[:, 7 * D + 1 * CPB * WW: 7 * D + 2 * CPB * WW]
        sa_c = lnxT[:, 7 * D + 2 * CPB * WW: 7 * D + 3 * CPB * WW]
        pfx = lnxT[:, 7 * D + 3 * CPB * WW: 7 * D + 4 * CPB * WW]
        x1 = lnxT[:, 0 * D:1 * D]
        xr_extra = lnxT[:, 7 * D + 4 * CPB * WW: 8 * D + 4 * CPB * WW]
        ln2 = lnxT[:, 1 * D:2 * D]
        abv = lnxT[:, 2 * D:3 * D]
        sgv = lnxT[:, 3 * D:4 * D]
        out2 = lnxT[:, 5 * D:6 * D]
        xr3 = [lnxT[:, 6 * D:7 * D], lnxT[:, 4 * D:5 * D], xr_extra]

        pb = [stack.enter_context(nc.psum_tensor(f"pb{i}", [128, 512], F32)) for i in range(8)]
        bank_readers = [[] for _ in range(8)]
        bank_writer = [None] * 8

        NSLOT = 8
        ld_slots = [f"ld{i}" for i in range(NSLOT)]
        lc_slots = [f"lc{i}" for i in range(4)]
        sp_slots = [f"sp{i}" for i in range(NSLOT)]
        S.setup(nc, stack, ld_slots + lc_slots + sp_slots + ["pe", "dv", "ac", "gp", "cc"])
        dma_idx = {"sync": 0, "gpsimd": 0, "scalar": 0}

        def _dma(eng, slots, dst, src, waits):
            i = dma_idx[eng]
            dma_idx[eng] += 1
            ch = slots[i % len(slots)]
            prev = S.cnt[ch]
            return S.run(eng, lambda e, d=dst, s=src: e.dma_start(out=d, in_=s),
                         list(waits) + [(ch, prev)], (ch, 16))

        def dma_in(dst, src, waits=()):
            return _dma("sync", ld_slots, dst, src, waits)

        def dma_in2(dst, src, waits=()):
            if cfg.get("no_scalar_dma"):
                return _dma("sync", ld_slots, dst, src, waits)
            return _dma("scalar", lc_slots, dst, src, waits)

        def dma_out(dst, src, waits=()):
            return _dma("gpsimd", sp_slots, dst, src, waits)

        def mm(bi, o, l, r, start, stop, tp=None, waits=(), skip=False):
            w = list(waits) + bank_readers[bi]
            bank_readers[bi] = []
            t = S.run("tensor",
                      lambda e, o=o, l=l, r=r, a=start, z=stop, t=tp, k=skip:
                      e.matmul(o, l, r, start=a, stop=z, tile_position=t, skip_group_check=k),
                      w, ("pe", 1))
            bank_writer[bi] = t
            return t

        def tr(bi, o, i, waits=()):
            w = list(waits) + bank_readers[bi]
            bank_readers[bi] = []
            t = S.run("tensor", lambda e, o=o, i=i: e.transpose(o, i, ident[0:i.shape[0], 0:i.shape[0]]),
                      w, ("pe", 1))
            bank_writer[bi] = t
            return t

        def vop(fn, waits=(), reads_bank=None):
            if reads_bank is not None:
                waits = list(waits) + [bank_writer[reads_bank]]
            t = S.run("vector", fn, waits, ("dv", 1))
            if reads_bank is not None:
                bank_readers[reads_bank].append(t)
            return t

        def aop(fn, waits=(), reads_bank=None):
            if reads_bank is not None:
                waits = list(waits) + [bank_writer[reads_bank]]
            t = S.run("scalar", fn, waits, ("ac", 1))
            if reads_bank is not None:
                bank_readers[reads_bank].append(t)
            return t

        def evac(dst, src, bi, waits=(), use="v"):
            if use == "v":
                return vop(lambda e, d=dst, s=src: e.tensor_copy(d, s), waits, reads_bank=bi)
            return aop(lambda e, d=dst, s=src: e.copy(d, s), waits, reads_bank=bi)

        def tr_slot(k):
            bk = k % 2
            bi = 7 if bk else 0
            sl = pb[bi][:, ((k // 2) % 2) * 128:(((k // 2) % 2) + 1) * 128]
            return bi, sl

        t_id = dma_in(ident[:, :], ident_in)
        t_p7 = vop(lambda e: e.memset(p7[:, :], 0.7))
        t_half = vop(lambda e: e.memset(halfv[:, :], 0.5))
        t_p7b = aop(lambda e: e.activation(p7i[:, :], p7[:, :], AF.Identity), [t_p7])
        t_pvp = aop(lambda e: e.activation(pv[:, :], p7i[:, :], AF.Identity), [t_p7b])
        t_mk = dma_in(maskL[:, :], maskL_in)
        t_pm = dma_in(pm[:, :], pm_in)
        t_wsc = dma_in(wscv[:, :], wsc_in)
        t_bias = dma_in(bias[:, :], bias_in)
        t_wf = []
        t_pk_prev = None
        NWCH = R * D // NPW
        for cw in range(NWCH):
            t_pk = dma_in(pk[:, :], pk_in[:, cw * NPW:(cw + 1) * NPW], [t_pk_prev])
            t_es = []
            for f in range(4):
                t_es.append(vop(lambda e, f=f: e.tensor_scalar(enc[32 * f:32 * (f + 1), :],
                                                               pk[32 * f:32 * (f + 1), :], 2 * f, 3,
                                                               ALU.logical_shift_right, ALU.bitwise_and),
                                [t_pk]))
            t_wf.append(vop(lambda e, cw=cw: e.tensor_scalar(wun[:, cw * NPW:(cw + 1) * NPW], enc[:, :],
                                                             1.0, None, ALU.subtract), t_es))
            t_pk_prev = t_wf[-1]
        tffn_toks = []
        si = 0
        for r in range(R):
            for ic in range(KC):
                bi, sl = tr_slot(si)
                t_t = tr(bi, sl, wun[:, r * D + ic * 128: r * D + ic * 128 + 128], [t_id] + t_wf)
                dst = tffn[:, ic * D + r: ic * D + r + R * 127 + 1: R]
                tffn_toks.append(evac(dst, sl, bi, [t_t], use=("v" if si % 2 else "a")))
                si += 1
        t_tffn = [max([t for t in tffn_toks if t[0] == "dv"], key=lambda x: x[1]),
                  max([t for t in tffn_toks if t[0] == "ac"], key=lambda x: x[1])]
        if cfg.get("debug7"):
            dbg7 = nc.dram_tensor("dbg7", [128, 2 * D], BF16, kind="ExternalOutput").ap()
            dma_out(dbg7[:, :], tffn[:, 0:2 * D], t_tffn)

        tile_tok = [None] * NT
        xt_free = [[], [], []]

        def tiles_tok(i0, i1):
            o_ = []
            for i in range(i0, i1):
                o_ += [tile_tok[i][0], tile_tok[i][1]]
            return o_

        t_wq = [dma_in2(wbig[:, kc * D:(kc + 1) * D], wq_in[kc * 128:(kc + 1) * 128, :])
                for kc in range(KC)]
        stg_free = [None, None]
        last_w_mm = None
        p2a_n = [0]

        def p2a_job(oc, tg, t_w, dsth):
            nonlocal last_w_mm
            par = p2a_n[0] % 2
            p2a_n[0] += 1
            bi = 2 + par
            t_m = None
            for kc in range(KC):
                t_m = mm(bi, pb[bi][:, :], wbig[:, kc * D + oc * 128: kc * D + oc * 128 + 128],
                         lnxT[:, kc * SL + tg * 512: kc * SL + tg * 512 + 512],
                         start=(kc == 0), stop=(kc == KC - 1),
                         waits=[t_w[kc]] + tiles_tok(tg * 4, tg * 4 + 4))
            last_w_mm = t_m
            t_ev = evac(stg[par][:, :], pb[bi][:, :], bi, [t_m, stg_free[par]])
            t_sp = dma_out(dsth[:, oc, tg * 512:(tg + 1) * 512], stg[par][:, :], [t_ev])
            stg_free[par] = t_sp

        wq_jobs = [(oc, tg) for tg in range(SL // 512) for oc in range(NJ)]
        wq_done = 0
        for i in range(NT):
            bi_ = i % 3
            t_x = dma_in(xt[bi_][:, :], x_in[i * 128:(i + 1) * 128, :], xt_free[bi_])
            t_s1 = vop(lambda e, b=bi_: e.tensor_reduce(nmv[:, :], xt[b][:, :], AX.X, ALU.add),
                       [t_x, tile_tok[i - 1][0] if i else None])
            t_nm = vop(lambda e: e.tensor_scalar(nmv[:, :], nmv[:, :], -1.0 / D, None, ALU.mult), [t_s1])
            t_sq = aop(lambda e, b=bi_: e.activation(sqd[:, :], xt[b][:, :], AF.Square, bias=nmv[:, :], scale=1.0,
                                                     accum_out=vsv[:, :]), [t_nm])
            t_v1 = vop(lambda e: e.tensor_scalar(vsv[:, :], vsv[:, :], 1.0 / D, 1e-5, ALU.mult, ALU.add), [t_sq])
            t_rr = vop(lambda e: e.reciprocal(rrv[:, :], vsv[:, :]), [t_v1])
            t_rs = aop(lambda e: e.activation(rsv[:, :], rrv[:, :], AF.Sqrt), [t_rr])
            t_n1 = vop(lambda e: e.tensor_tensor(nwa[:, :], rsv[:, :], rsv[:, :], ALU.mult), [t_rs])
            t_n2 = vop(lambda e: e.tensor_tensor(nwa[:, :], nwa[:, :], vsv[:, :], ALU.mult), [t_n1])
            t_n3 = vop(lambda e: e.tensor_scalar(nwa[:, :], nwa[:, :], -0.5, 1.5, ALU.mult, ALU.add), [t_n2])
            t_n4 = vop(lambda e: e.tensor_tensor(rsv[:, :], rsv[:, :], nwa[:, :], ALU.mult), [t_n3])
            t_ln = vop(lambda e, b=bi_: e.tensor_scalar(xt[b][:, :], xt[b][:, :], nmv[:, :], rsv[:, :],
                                                        ALU.add, ALU.mult), [t_n4])
            toks = []
            for kc in range(KC):
                bi2, sl = tr_slot(kc)
                t_t = tr(bi2, sl, xt[bi_][:, kc * 128:(kc + 1) * 128], [t_ln, t_id])
                dst = lnxT[:, kc * SL + i * 128: kc * SL + (i + 1) * 128]
                toks.append(evac(dst, sl, bi2, [t_t], use=("v" if kc % 2 == 0 else "a")))
            dmax = max([t for t in toks if t[0] == "dv"], key=lambda x: x[1], default=None)
            amax = max([t for t in toks if t[0] == "ac"], key=lambda x: x[1], default=None)
            tile_tok[i] = (dmax, amax)
            xt_free[bi_] = [dmax, amax]
            if i >= 5:
                budget = 1
                while (wq_done < len(wq_jobs) and budget > 0
                       and wq_jobs[wq_done][1] * 4 + 4 <= i + 1):
                    oc_, tg_ = wq_jobs[wq_done]
                    p2a_job(oc_, tg_, t_wq, qt_h)
                    wq_done += 1
                    budget -= 1

        for oc_, tg_ in wq_jobs[wq_done:]:
            p2a_job(oc_, tg_, t_wq, qt_h)
        t_wk = []
        for kc in range(KC):
            t_wk.append(dma_in(wbig[:, kc * D:(kc + 1) * D], wk_in[kc * 128:(kc + 1) * 128, :],
                               [last_w_mm]))
        for oc in range(NJ):
            for tg in range(SL // 512):
                p2a_job(oc, tg, t_wk, kt_h)

        t_wv = []
        for kc in range(KC):
            t_wv.append(dma_in(wbig[:, kc * D:(kc + 1) * D], wv_in[kc * 128:(kc + 1) * 128, :], [last_w_mm]))
        for i in range(NT):
            for nh in range(NH):
                par = (i * NH + nh) % 2
                bi = 2 + par
                t_m = None
                for kc in range(KC):
                    t_m = mm(bi, pb[bi][:, :], lnxT[:, kc * SL + i * 128: kc * SL + (i + 1) * 128],
                             wbig[:, kc * D + nh * 512: kc * D + nh * 512 + 512],
                             start=(kc == 0), stop=(kc == KC - 1),
                             waits=[t_wv[kc]] + tiles_tok(i, i + 1))
                last_w_mm = t_m
                t_ev = evac(stg[par][:, :], pb[bi][:, :], bi, [t_m, stg_free[par]])
                t_sp = dma_out(v_h[i * 128:(i + 1) * 128, nh * 512:(nh + 1) * 512], stg[par][:, :], [t_ev])
                stg_free[par] = t_sp
        t_spills_l = [stg_free[0], stg_free[1]]

        SA = cfg.get("stop_after", 9)
        if SA < 3:
            S.run("gpsimd", None, waits=[(ch, S.cnt[ch]) for ch in sp_slots])
            with nc.Block() as block:
                S.emit_into(block)
            return nc
        t_z1 = vop(lambda e: e.memset(s1run[:, :], 0.0))
        t_z2 = vop(lambda e: e.memset(hloc[:, :], 0.0))
        t_z3 = vop(lambda e: e.memset(crun[:, :], 0.0))
        s1_tok = t_z1
        hl_tok = t_z2
        cr_tok = t_z3
        u3 = {"qtc": [[], []], "ktc": [[], []], "vc": [[], []],
              "ktok": [[], []], "qtok": [[], []]}
        loads3 = [None] * (NT + 1)

        def p3_load(c):
            p_ = c % 2
            ex = t_spills_l if c < 2 else []
            tq = dma_in(qtc2[p_][:, :], qt_h[:, :, c * 128:(c + 1) * 128], ex + u3["qtc"][p_])
            tk = dma_in(ktc2[p_][:, :], kt_h[:, :, c * 128:(c + 1) * 128], ex + u3["ktc"][p_])
            tv = dma_in2(vc2[p_][:, :], v_h[c * 128:(c + 1) * 128, :], ex + u3["vc"][p_])
            u3["qtc"][p_] = []; u3["ktc"][p_] = []; u3["vc"][p_] = []
            return (tq, tk, tv)

        loads3[0] = p3_load(0)
        g1_sp_par = [None, None]
        g2c_rd = [[], []]
        g1c_rd = [[], []]
        for c in range(NT):
            p = c % 2
            t_qt, t_kt, t_vl = loads3[c]
            if c + 1 < NT:
                loads3[c + 1] = p3_load(c + 1)
            qtc = qtc2[p]; ktc = ktc2[p]; vc = vc2[p]
            ktok = ktok2[p]; qtok = qtok2[p]
            g1c = g1c2[p]; g2c = g2c2[p]
            g1c_rd_old = g1c_rd[p]
            g1c_rd[p] = []
            g2c_rd_old = g2c_rd[p]
            g2c_rd[p] = []
            old_ktok_u = u3["ktok"][p]; u3["ktok"][p] = []
            old_qtok_u = u3["qtok"][p]; u3["qtok"][p] = []
            ktok_toks = [None] * NJ
            qtok_toks = [None] * NJ
            for jj in range(NJ):
                bi, sl = tr_slot(jj)
                t_t = tr(bi, sl, ktc[:, jj * 128:(jj + 1) * 128], [t_kt] + old_ktok_u)
                u3["ktc"][p].append(t_t)
                ktok_toks[jj] = evac(ktok[:, jj * 128:(jj + 1) * 128], sl, bi, [t_t],
                                     use=("v" if jj % 2 else "a"))
            for jj in range(NJ):
                bi, sl = tr_slot(jj + NJ)
                t_t = tr(bi, sl, qtc[:, jj * 128:(jj + 1) * 128], [t_qt] + old_qtok_u)
                u3["qtc"][p].append(t_t)
                qtok_toks[jj] = evac(qtok[:, jj * 128:(jj + 1) * 128], sl, bi, [t_t],
                                     use=("v" if jj % 2 else "a"))

            am_toks = [None] * NJ

            def stage_A(jj):
                jw = slice(jj * 128, (jj + 1) * 128)
                res = []
                for hh, b in ((0, 0), (1, 64)):
                    bs = slice(b, b + 64)
                    bi = 4 - hh
                    t_as = mm(bi, pb[bi][:, 0:128], ktc[bs, jw], qtc[bs, jw], True, True,
                              tp=(b, 0), waits=[t_kt, t_qt])
                    u3["qtc"][p].append(t_as); u3["ktc"][p].append(t_as)
                    ai = (jj % 2) * 2 + hh
                    res.append(vop(lambda e, a=ai, bb=bi: e.tensor_tensor(asm[a][:, :], pb[bb][:, 0:128],
                                                                         maskL[:, :], ALU.mult),
                                   [t_as, t_mk], reads_bank=bi))
                am_toks[jj] = res

            g1_evs, g2_evs = [], []
            corr_toks = []
            dS1_mms = []
            dH_mms = []

            def stage_G(jj):
                jw = slice(jj * 128, (jj + 1) * 128)
                jw64 = slice(jj * 64, (jj + 1) * 64)
                t_am = am_toks[jj]
                for hh, b in ((0, 0), (1, 64)):
                    bs = slice(b, b + 64)
                    hc = slice((2 * jj + hh) * 64, (2 * jj + hh) * 64 + 64)
                    ai = (jj % 2) * 2 + hh
                    t_a = mm(5, pb[5][bs, 0:128], ktok[:, hc], asm[ai][:, :], True, False, tp=(0, b),
                             waits=[ktok_toks[jj], t_am[hh]], skip=True)
                    u3["ktok"][p].append(t_a)
                t_b = mm(5, pb[5][:, 0:128], s1run[:, jw], qtc[:, jw], False, True, tp=(0, 0),
                         waits=[s1_tok], skip=True)
                u3["qtc"][p].append(t_b)
                corr_toks.append(t_b)
                t_g1ev = vop(lambda e, d=g1c[:, jw]: e.tensor_copy(d, pb[5][:, 0:128]),
                             [t_b, g1_sp_par[p]] + g1c_rd_old, reads_bank=5)
                g1_evs.append(t_g1ev)
                bi6, sl6 = tr_slot(2 * jj + (c % 2))
                t_t2 = tr(bi6, sl6, g1c[:, jw], [t_g1ev])
                g1c_rd[p].append(t_t2)
                t_g2ev = evac(g2c[:, jw], sl6, bi6, [t_t2] + g2c_rd_old, use="a")
                g2_evs.append(t_g2ev)
                for hh, b in ((0, 0), (1, 64)):
                    hc = slice((2 * jj + hh) * 64, (2 * jj + hh) * 64 + 64)
                    t_d = mm(1, pb[1][slice(b, b + 64), jw64], ktok[:, hc], ktok[:, hc], True, True,
                             tp=(0, b), waits=[ktok_toks[jj]])
                    u3["ktok"][p].append(t_d)
                    dS1_mms.append(t_d)
                for hh, b in ((0, 0), (1, 64)):
                    hc = slice((2 * jj + hh) * 64, (2 * jj + hh) * 64 + 64)
                    t_d = mm(2, pb[2][slice(b, b + 64), jw64], g2c[:, hc], vc[:, hc], True, True,
                             tp=(0, b), waits=[t_g2ev, t_vl])
                    u3["vc"][p].append(t_d)
                    g2c_rd[p].append(t_d)
                    dH_mms.append(t_d)

            stage_A(0)
            for jj in range(NJ):
                if jj + 1 < NJ:
                    stage_A(jj + 1)
                stage_G(jj)

            s1e_o = s1run[0:64, :].rearrange("p (j c) -> p j c", c=128)[:, :, 0:64]
            s1e_i = pb[1][0:64, 0:NJ * 64].rearrange("p (j c) -> p j c", c=64)
            s1o_o = s1run[64:128, :].rearrange("p (j c) -> p j c", c=128)[:, :, 64:128]
            s1o_i = pb[1][64:128, 0:NJ * 64].rearrange("p (j c) -> p j c", c=64)
            sa1a = vop(lambda e: e.tensor_tensor(s1e_o, s1e_i, s1e_o, ALU.add),
                       [s1_tok] + dS1_mms + corr_toks, reads_bank=1)
            sa1 = vop(lambda e: e.tensor_tensor(s1o_o, s1o_i, s1o_o, ALU.add),
                      [sa1a], reads_bank=1)
            s1_tok = sa1
            ha1 = vop(lambda e: e.tensor_tensor(hloc[:, :], pb[2][:, 0:NJ * 64], hloc[:, :], ALU.add),
                      [hl_tok] + dH_mms, reads_bank=2)
            hl_tok = ha1
            dC_mms = []
            for h in range(H):
                jj, hh = h // 2, h % 2
                hc = slice(h * 64, h * 64 + 64)
                t_d = mm(1, pb[1][slice(64 * hh, 64 * hh + 64), jj * 64:(jj + 1) * 64],
                         qtok[:, hc], vc[:, hc], True, True, tp=(0, 64 * hh),
                         waits=[qtok_toks[jj], t_vl])
                u3["qtok"][p].append(t_d)
                u3["vc"][p].append(t_d)
                dC_mms.append(t_d)
            ca1 = vop(lambda e: e.tensor_tensor(crun[:, :], pb[1][:, 0:NJ * 64], crun[:, :], ALU.add),
                      [cr_tok] + dC_mms, reads_bank=1)
            cr_tok = ca1
            g1_sp_par[p] = dma_out(g1_h[c, :, :], g1c[:, :], g1_evs)

        if SA < 4:
            S.run("gpsimd", None, waits=[(ch, S.cnt[ch]) for ch in sp_slots])
            with nc.Block() as block:
                S.emit_into(block)
            return nc
        u5 = {"qtc": [[], []], "vc": [[], []], "g1": [[], []], "g2": [[], []], "otc": [[], [], []]}
        u5["qtc"][0] = [t for t in u3["qtc"][0] if t is not None]
        u5["qtc"][1] = [t for t in u3["qtc"][1] if t is not None]
        u5["vc"][0] = u3["vc"][0]; u5["vc"][1] = u3["vc"][1]
        u5["g1"][0] = [g1_sp_par[0]] + g1c_rd[0]
        u5["g1"][1] = [g1_sp_par[1]] + g1c_rd[1]

        loads5 = [None] * (NT + 1)

        def p5_load(c):
            p_ = c % 2
            tq = dma_in(qtc2[p_][:, :], qt_h[:, :, c * 128:(c + 1) * 128], u5["qtc"][p_])
            tv = dma_in2(vc2[p_][:, :], v_h[c * 128:(c + 1) * 128, :], u5["vc"][p_])
            tg1 = dma_in(g1c2[p_][:, :], g1_h[c, :, :], u5["g1"][p_])
            u5["qtc"][p_] = []; u5["vc"][p_] = []; u5["g1"][p_] = []
            return (tq, tv, tg1, None)

        loads5[0] = p5_load(0)
        if NT > 1:
            loads5[1] = p5_load(1)

        tp_e = tmpst[0:64, :].rearrange("p (j c) -> p j c", c=64)
        tp_o = tmpst[64:128, :].rearrange("p (j c) -> p j c", c=64)
        s1bde = s1run[0:64, :].rearrange("p (j c) -> p j c", c=128)[:, :, 0:64]
        s1bdo = s1run[64:128, :].rearrange("p (j c) -> p j c", c=128)[:, :, 64:128]
        t_x1e = vop(lambda e: e.tensor_copy(tp_e, s1bde), [s1_tok])
        t_x1o = vop(lambda e: e.tensor_copy(tp_o, s1bdo), [t_x1e])
        t_sm0 = dma_out(summ_t.ap()[0:128, :], tmpst[:, :], [t_x1o])
        t_sm1 = dma_out(summ_t.ap()[128:256, :], hloc[:, :], [hl_tok])
        t_sm2 = dma_out(summ_t.ap()[256:384, :], crun[:, :], [cr_tok])
        groups = [list(range(g * CPB, (g + 1) * CPB)) for g in range(B)]
        t_cc = S.run("gpsimd",
                     lambda e: e.collective_compute("AllGather", ALU.bypass, replica_groups=groups,
                                                    ins=[summ_t.ap().opt()], outs=[summ_all_t.ap().opt()]),
                     [t_sm0, t_sm1, t_sm2], ("cc", 1))
        sall3 = summ_all_t.ap().rearrange("(i k p) w -> i k p w", k=3, p=128)
        t_la = []
        for kind, dst in ((0, sa_a), (1, sa_b), (2, sa_c)):
            t_la.append(dma_in(dst.rearrange("p (i w) -> p i w", i=CPB),
                               sall3[:, kind, :, :].rearrange("i p w -> p i w"), [t_cc]))
        t_pz = vop(lambda e: e.memset(pfx[:, 0:WW], 0.0))
        t_pv = [t_pz]
        for i in range(1, CPB):
            t_pv.append(vop(lambda e, i=i: e.tensor_tensor(pfx[:, i * WW:(i + 1) * WW], pfx[:, (i - 1) * WW:i * WW],
                                                           sa_a[:, (i - 1) * WW:i * WW], ALU.add), [t_pv[-1]] + t_la))
        t_s1z = vop(lambda e: e.memset(s1in[:, :], 0.0))
        t_acc = t_s1z
        for i in range(CPB):
            t_t1 = vop(lambda e, i=i: e.tensor_scalar(tmpst[:, :], sa_a[:, i * WW:(i + 1) * WW], pm[:, i:i + 1], None,
                                                      ALU.mult), t_la + [t_pm, t_acc, t_sm0])
            t_acc = vop(lambda e: e.tensor_tensor(s1in[:, :], s1in[:, :], tmpst[:, :], ALU.add), [t_t1, t_acc])
        t_s1in = t_acc
        t_bz = vop(lambda e: e.memset(s1bd[:, :], 0.0))
        t_bd = [t_bz]
        for jj in range(NJ):
            t_bd.append(vop(lambda e, jj=jj: e.tensor_copy(s1bd[0:64, jj * 128:jj * 128 + 64],
                                                           s1in[0:64, jj * 64:(jj + 1) * 64]), [t_s1in, t_bz]))
            t_bd.append(vop(lambda e, jj=jj: e.tensor_copy(s1bd[64:128, jj * 128 + 64:jj * 128 + 128],
                                                           s1in[64:128, jj * 64:(jj + 1) * 64]), [t_s1in, t_bz]))
        t_s1bd = [t_bd[-1]]
        t_hz = vop(lambda e: e.memset(hin[:, :], 0.0))
        t_hacc = t_hz
        prev_pbd_users = []
        for i in range(CPB):
            if i == 0:
                t_t1 = vop(lambda e: e.tensor_scalar(tmpst[:, :], sa_b[:, 0:WW], pm[:, 0:1], None, ALU.mult),
                           t_la + [t_pm, t_acc, t_hacc])
                t_hacc = vop(lambda e: e.tensor_tensor(hin[:, :], hin[:, :], tmpst[:, :], ALU.add), [t_t1, t_hacc])
                continue
            t_pbz = vop(lambda e: e.memset(pbd[:, :], 0.0), prev_pbd_users)
            t_pb = [t_pbz]
            for jj in range(NJ):
                t_pb.append(vop(lambda e, jj=jj, i=i: e.tensor_copy(pbd[0:64, jj * 128:jj * 128 + 64],
                                                                   pfx[0:64, i * WW + jj * 64: i * WW + (jj + 1) * 64]),
                                [t_pv[i], t_pbz]))
                t_pb.append(vop(lambda e, jj=jj, i=i: e.tensor_copy(pbd[64:128, jj * 128 + 64:jj * 128 + 128],
                                                                   pfx[64:128, i * WW + jj * 64: i * WW + (jj + 1) * 64]),
                                [t_pv[i], t_pbz]))
            t_x = None
            prev_pbd_users = []
            for jj in range(NJ):
                t_x = mm(2, pb[2][:, jj * 64:(jj + 1) * 64], pbd[:, jj * 128:(jj + 1) * 128],
                         sa_c[:, i * WW + jj * 64: i * WW + (jj + 1) * 64], True, True,
                         waits=t_pb + t_la)
                prev_pbd_users.append(t_x)
            t_t1 = vop(lambda e, i=i: e.tensor_tensor(tmpst[:, :], sa_b[:, i * WW:(i + 1) * WW], pb[2][:, 0:WW], ALU.add),
                       [t_x, t_hacc], reads_bank=2)
            t_t2 = vop(lambda e, i=i: e.tensor_scalar(tmpst[:, :], tmpst[:, :], pm[:, i:i + 1], None, ALU.mult), [t_t1])
            t_hacc = vop(lambda e: e.tensor_tensor(hin[:, :], hin[:, :], tmpst[:, :], ALU.add), [t_t2])
        t_hbz = vop(lambda e: e.memset(hbd[:, :], 0.0), [t_hacc] if False else [])
        t_hb = [t_hbz]
        for jj in range(NJ):
            t_hb.append(vop(lambda e, jj=jj: e.tensor_copy(hbd[0:64, jj * 128:jj * 128 + 64],
                                                           hin[0:64, jj * 64:(jj + 1) * 64]), [t_hacc, t_hbz]))
            t_hb.append(vop(lambda e, jj=jj: e.tensor_copy(hbd[64:128, jj * 128 + 64:jj * 128 + 128],
                                                           hin[64:128, jj * 64:(jj + 1) * 64]), [t_hacc, t_hbz]))
        t_hin = t_hb[-1]

        if cfg.get("debug"):
            dbg = nc.dram_tensor("dbg", [5 * 128, NJ * 64], F32, kind="ExternalOutput").ap()
            dma_out(dbg[0 * 128:1 * 128, :], s1run[:, :], [s1_tok])
            dma_out(dbg[1 * 128:2 * 128, :], hloc[:, :], [hl_tok])
            dma_out(dbg[2 * 128:3 * 128, :], crun[:, :], [cr_tok])
            dma_out(dbg[3 * 128:4 * 128, :], s1in[:, :], [t_s1in])
            t_hin = dma_out(dbg[4 * 128:5 * 128, :], hin[:, :], [t_hin])

        t_wo = []
        for jj in range(NJ):
            t_wo.append(dma_in(wbig[:, jj * D:(jj + 1) * D], wo_in[jj * 128:(jj + 1) * 128, :],
                               [last_w_mm, cr_tok, s1_tok, hl_tok]))

        if SA < 5:
            S.run("gpsimd", None, waits=[(ch, S.cnt[ch]) for ch in sp_slots])
            with nc.Block() as block:
                S.emit_into(block)
            return nc
        hr_tok = t_hin
        t_xr_tok = [None] * NT
        xr_free = [list(t_spills_l), list(t_spills_l), list(t_spills_l)]
        out_free = [None] * NH
        P5N = cfg.get("p5_chunks", NT)
        epi_stageq = []

        def epi_step():
            if epi_stageq:
                epi_stageq.pop(0)()
        for c in range(NT + 1):
            if c < P5N:
                p = c % 2
                t_qt, t_vl, t_g1, t_g2 = loads5[c]
                if c + 1 < P5N and loads5[c + 1] is None:
                    loads5[c + 1] = p5_load(c + 1)
                qtc = qtc2[p]; vc = vc2[p]; g1c = g1c2[p]; g2c = g2c2[p]
                p3_ = c % 3
                otc = otc3[p3_]
                t_xr = dma_in2(xr3[p3_], x_in[c * 128:(c + 1) * 128, :], xr_free[p3_])
                t_xr_tok[c] = t_xr

                gf_toks = [None] * NJ
                ot_evs = []
                dH_mms5 = []
                m6b_toks = []

                def stage_MA(jj):
                    jw = slice(jj * 128, (jj + 1) * 128)
                    bi = 2 + (jj % 2)
                    t_qs = mm(bi, pb[bi][:, 0:128], s1bd[:, jw], qtc[:, jw], True, True,
                              waits=t_s1bd + [t_qt])
                    u5["qtc"][p].append(t_qs)
                    gp = jj % 2
                    t_g1f = vop(lambda e, w=jw, g=gp, bb=bi, src=g1c: e.tensor_tensor(g1f2[g][:, :], pb[bb][:, 0:128],
                                                                            src[:, w], ALU.add),
                                [t_qs, t_g1] + ([gf_toks[jj - 2][2]] if jj >= 2 else []), reads_bank=bi)
                    u5["g1"][p].append(t_g1f)
                    bi7, sl7 = tr_slot(jj)
                    t_tg = tr(bi7, sl7, g1f2[gp][:, :], [t_g1f])
                    t_g2f = evac(g2f2[gp][:, :], sl7, bi7,
                                 [t_tg] + ([gf_toks[jj - 2][3]] if jj >= 2 else []), use="a")
                    gf_toks[jj] = [t_g1f, t_g2f, None, None]

                def stage_MB(jj):
                    jw = slice(jj * 128, (jj + 1) * 128)
                    jw64 = slice(jj * 64, (jj + 1) * 64)
                    gp = jj % 2
                    t_g1f, t_g2f = gf_toks[jj][0], gf_toks[jj][1]
                    t_bm = [None, None]
                    for hh, b in ((0, 0), (1, 64)):
                        bs = slice(b, b + 64)
                        bsb = 4 + hh
                        t_bs = mm(bsb, pb[bsb][:, 0:128], g1f2[gp][bs, :], qtc[bs, jw], True, True,
                                  tp=(b, 0), waits=[t_g1f])
                        u5["qtc"][p].append(t_bs)
                        ai = (jj % 2) * 2 + hh
                        t_bm[hh] = vop(lambda e, a=ai, bb=bsb: e.tensor_tensor(asm[a][:, :], pb[bb][:, 0:128],
                                                                              maskL[:, :], ALU.mult),
                                       [t_bs, t_mk], reads_bank=bsb)
                    for hh, b in ((0, 0), (1, 64)):
                        bs = slice(b, b + 64)
                        hc = slice((2 * jj + hh) * 64, (2 * jj + hh) * 64 + 64)
                        ai = (jj % 2) * 2 + hh
                        t_oa = mm(6, pb[6][bs, 0:128], vc[:, hc], asm[ai][:, :], True, False, tp=(0, b),
                                  waits=[t_vl, t_bm[hh]], skip=True)
                        u5["vc"][p].append(t_oa)
                    t_last = mm(6, pb[6][:, 0:128], hbd[:, jw], qtc[:, jw], False, True, tp=(0, 0),
                                waits=[hr_tok], skip=True)
                    u5["qtc"][p].append(t_last)
                    m6b_toks.append(t_last)
                    for hh, b in ((0, 0), (1, 64)):
                        hc = slice((2 * jj + hh) * 64, (2 * jj + hh) * 64 + 64)
                        t_d = mm(1, pb[1][slice(b, b + 64), jw64], g2f2[gp][:, hh * 64:hh * 64 + 64],
                                 vc[:, hc], True, True, tp=(0, b), waits=[t_g2f, t_vl])
                        u5["vc"][p].append(t_d)
                        dH_mms5.append(t_d)
                        gf_toks[jj][3] = t_d
                    t_oe = evac(otc[:, jw], pb[6][:, 0:128], 6,
                                [t_last] + u5["otc"][p3_], use=("v" if jj % 2 else "a"))
                    if jj == 0:
                        u5["otc"][p3_] = []
                    gf_toks[jj][2] = t_oe
                    ot_evs.append(t_oe)

                if cfg.get("p5_only_loads"):
                    epi_pending = None
                    continue
                stage_MA(0)
                if NJ > 1:
                    stage_MA(1)
                epi_step()
                epi_step()
                if cfg.get("p5_ma_only"):
                    for jj in range(2, NJ):
                        stage_MA(jj)
                    epi_pending = None
                    continue
                for jj in range(NJ):
                    stage_MB(jj)
                    if jj + 2 < NJ:
                        stage_MA(jj + 2)
                    epi_step()
                hb_e = hbd[0:64, :].rearrange("p (j c) -> p j c", c=128)[:, :, 0:64]
                hb_ei = pb[1][0:64, 0:NJ * 64].rearrange("p (j c) -> p j c", c=64)
                hb_o = hbd[64:128, :].rearrange("p (j c) -> p j c", c=128)[:, :, 64:128]
                hb_oi = pb[1][64:128, 0:NJ * 64].rearrange("p (j c) -> p j c", c=64)
                hr_a = vop(lambda e: e.tensor_tensor(hb_e, hb_ei, hb_e, ALU.add),
                           [hr_tok] + dH_mms5 + m6b_toks, reads_bank=1)
                hr_tok = vop(lambda e: e.tensor_tensor(hb_o, hb_oi, hb_o, ALU.add),
                             [hr_a], reads_bank=1)
                epi_step()
                epi_step()

                def make_epi(c=c, p=p3_, ot_evs=tuple(ot_evs), t_xr=t_xr):
                    otc_ = otc3[p]
                    xr_ = xr3[p]
                    st = {}

                    def wo_nh(nh):
                        def f():
                            t_m = None
                            for jj in range(NJ):
                                t_m = mm(7, pb[7][:, :], otc_[:, jj * 128:(jj + 1) * 128],
                                         wbig[:, jj * D + nh * 512: jj * D + nh * 512 + 512],
                                         start=(jj == 0), stop=(jj == NJ - 1),
                                         waits=[t_wo[jj]] + list(ot_evs))
                                u5["otc"][p].append(t_m)
                            t_x1 = vop(lambda e, n=nh, xv=xr_: e.tensor_tensor(x1[:, n * 512:(n + 1) * 512], pb[7][:, :],
                                                                       xv[:, n * 512:(n + 1) * 512], ALU.add),
                                       [t_m, t_xr, out_free[nh]], reads_bank=7)
                            st.setdefault("x1ev", []).append(t_x1)
                        return f

                    def lnq_p1():
                        x1ev = st["x1ev"]
                        t_s1 = aop(lambda e: e.activation(sqd5[:, :], x1, AF.Identity, accum_out=nmv[:, :]), x1ev)
                        st["nm"] = vop(lambda e: e.tensor_scalar(nmv[:, :], nmv[:, :], -1.0 / D, None, ALU.mult), [t_s1])

                    def lnq_p2():
                        t_sq = aop(lambda e: e.activation(sqd5[:, :], x1, AF.Square, bias=nmv[:, :], scale=1.0,
                                                          accum_out=vsv[:, :]), [st["nm"]])
                        t_v1 = vop(lambda e: e.tensor_scalar(vsv[:, :], vsv[:, :], 1.0 / D, 1e-5, ALU.mult, ALU.add), [t_sq])
                        t_rr = vop(lambda e: e.reciprocal(rrv[:, :], vsv[:, :]), [t_v1])
                        st["rs"] = aop(lambda e: e.activation(rsv[:, :], rrv[:, :], AF.Sqrt), [t_rr])

                    def lnq_p3():
                        t_n1 = vop(lambda e: e.tensor_tensor(nwa[:, :], rsv[:, :], rsv[:, :], ALU.mult), [st["rs"]])
                        t_n2 = vop(lambda e: e.tensor_tensor(nwa[:, :], nwa[:, :], vsv[:, :], ALU.mult), [t_n1])
                        t_n3 = vop(lambda e: e.tensor_scalar(nwa[:, :], nwa[:, :], -0.5, 1.5, ALU.mult, ALU.add), [t_n2])
                        t_n4 = vop(lambda e: e.tensor_tensor(rsv[:, :], rsv[:, :], nwa[:, :], ALU.mult), [t_n3])
                        st["ms"] = vop(lambda e: e.tensor_tensor(mrs[:, :], nmv[:, :], rsv[:, :], ALU.mult), [t_n4])

                    def lnq_p4():
                        t_ln = aop(lambda e: e.activation(ln2, x1, AF.Identity, scale=rsv[:, :], bias=mrs[:, :]), [st["ms"]])
                        st["mx"] = vop(lambda e: e.tensor_reduce(mxv[:, :], ln2, AX.X, ALU.max, apply_absolute_value=True), [t_ln])

                    def lnq_p5():
                        t_mx2 = vop(lambda e: e.tensor_scalar(mxv[:, :], mxv[:, :], 1e-5, None, ALU.max), [st["mx"]])
                        t_sc = vop(lambda e: e.reciprocal(scv[:, :], mxv[:, :]), [t_mx2])
                        t_sc2 = vop(lambda e: e.tensor_scalar(scv[:, :], scv[:, :], 127.0, None, ALU.mult), [t_sc])
                        st["y"] = aop(lambda e: e.activation(abv, ln2, AF.Identity, scale=scv[:, :]), [t_sc2])
                        t_e1 = vop(lambda e: e.tensor_tensor(epv[:, :], mxv[:, :], wscv[:, :], ALU.mult), [t_mx2, t_wsc])
                        st["e2"] = vop(lambda e: e.tensor_scalar(epv[:, :], epv[:, :], 1.0 / 127.0, None, ALU.mult), [t_e1])

                    def lnq_p6():
                        t_r1 = aop(lambda e: e.activation(q32[:, :], abv, AF.Identity), [st["y"]])
                        st["r2"] = aop(lambda e: e.activation(q32b[:, :], abv, AF.Identity, bias=halfv[:, :]),
                                       [t_r1, t_half])

                    def lnq_p7():
                        t_b1 = vop(lambda e: e.tensor_tensor(sgv, q32[:, :], q32b[:, :], ALU.subtract), [st["r2"]])
                        t_b2 = vop(lambda e: e.tensor_scalar(sgv, sgv, pv[:, :], None, ALU.mult), [t_b1, t_pvp])
                        st["qf"] = vop(lambda e: e.tensor_tensor(abv, q32b[:, :], sgv, ALU.add), [t_b2])

                    def lnq_p8():
                        tq_toks = []
                        for kc in range(KC):
                            sl = pb[0][:, (kc % 4) * 128:((kc % 4) + 1) * 128]
                            t_t = tr(0, sl, abv[:, kc * 128:(kc + 1) * 128], [st["qf"]])
                            tq_toks.append(evac(qxtb[:, kc * 128:(kc + 1) * 128], sl, 0, [t_t],
                                                use=("v" if kc % 2 else "a")))
                        st["tq"] = tq_toks

                    def ffn_nh(nh):
                        def f():
                            t_m = None
                            for kc in range(KC):
                                t_m = mm(7, pb[7][:, :], qxtb[:, kc * 128:(kc + 1) * 128],
                                         tffn[:, kc * D + nh * 512: kc * D + nh * 512 + 512],
                                         start=(kc == 0), stop=(kc == KC - 1),
                                         waits=t_tffn + st["tq"])
                            t_f1 = vop(lambda e, n=nh: e.tensor_scalar(out2[:, n * 512:(n + 1) * 512], pb[7][:, :],
                                                                       epv[:, :], None, ALU.mult),
                                       [t_m, st["e2"], out_free[nh]], reads_bank=7)
                            t_f2 = vop(lambda e, n=nh: e.tensor_tensor(out2[:, n * 512:(n + 1) * 512],
                                                                       out2[:, n * 512:(n + 1) * 512],
                                                                       bias[:, n * 512:(n + 1) * 512], ALU.add),
                                       [t_f1, t_bias])
                            t_f3 = vop(lambda e, n=nh: e.tensor_tensor(out2[:, n * 512:(n + 1) * 512],
                                                                       out2[:, n * 512:(n + 1) * 512],
                                                                       x1[:, n * 512:(n + 1) * 512], ALU.add), [t_f2])
                            t_st = dma_out(out[c * 128:(c + 1) * 128, nh * 512:(nh + 1) * 512],
                                           out2[:, nh * 512:(nh + 1) * 512], [t_f3])
                            out_free[nh] = t_st
                            st.setdefault("sts", []).append(t_st)
                            if nh == NH - 1:
                                xr_free[p] = [t_st]
                            if cfg.get("debug5") and c == 0 and nh == NH - 1:
                                dbg5 = nc.dram_tensor("dbg5", [3 * 128, D], F32, kind="ExternalOutput").ap()
                                dma_out(dbg5[0:128, :], x1, [t_st])
                                dma_out(dbg5[128:256, :], out2[:, :], [t_st])
                                t_o5 = vop(lambda e: e.tensor_copy(sgv, qxtb[:, :]), [t_st])
                                dma_out(dbg5[256:384, :], sgv, [t_o5])
                                dbg6 = nc.dram_tensor("dbg6", [128, 2 * D], F32, kind="ExternalOutput").ap()
                                t_o6 = vop(lambda e: e.tensor_copy(x1, tffn[:, 0:D]), [t_st, t_o5])
                                dma_out(dbg6[:, 0:D], x1, [t_o6])
                                t_o7 = vop(lambda e: e.tensor_copy(x1, tffn[:, D:2 * D]), [t_o6])
                                dma_out(dbg6[:, D:2 * D], x1, [t_o7])
                        return f

                    stages = [wo_nh(0)]
                    for nh in range(1, NH):
                        stages.append(wo_nh(nh))
                    stages += [lnq_p1, lnq_p2, lnq_p3, lnq_p4, lnq_p5, lnq_p6, lnq_p7, lnq_p8]
                    for nh in range(NH):
                        stages.append(ffn_nh(nh))
                    return stages

                if not cfg.get("p5_no_epi"):
                    epi_stageq.extend(make_epi())
                if c == P5N - 1:
                    while epi_stageq:
                        epi_step()

        S.run("gpsimd", None, waits=[(ch, S.cnt[ch]) for ch in sp_slots])

        with nc.Block() as block:
            S.emit_into(block)

    return nc


def make_inputs(cfg, full):
    D = cfg["D"]; SL = cfg["SL"]; B = cfg["B"]; NCORE = cfg["NCORE"]
    CPB = NCORE // B
    x = np.ascontiguousarray(full["x"], dtype=np.float32)
    ident = np.eye(128, dtype=np.float32)
    maskL = (np.arange(128)[:, None] <= np.arange(128)[None, :]).astype(np.float32)
    bias_rep = np.ascontiguousarray(np.broadcast_to(full["bias"].astype(np.float32).reshape(1, D), (128, D)))
    wsc_rep = np.full((128, 1), np.float32(np.asarray(full["wscale"]).reshape(-1)[0]), dtype=np.float32)
    RD = D * D // 128
    pkw = np.asarray(full["packed"]).astype(np.int32).reshape(32, RD)
    pk = np.ascontiguousarray(pkw[np.arange(128) % 32])
    wq = np.ascontiguousarray(full["Wq"], dtype=np.float32)
    wk = np.ascontiguousarray(full["Wk"], dtype=np.float32)
    wv = np.ascontiguousarray(full["Wv"], dtype=np.float32)
    wo = np.ascontiguousarray(full["Wo"], dtype=np.float32)
    in_maps = []
    for c in range(NCORE):
        b, s = c // CPB, c % CPB
        pmask = np.zeros((128, CPB), dtype=np.float32)
        pmask[:, :s] = 1.0
        in_maps.append({
            "x_in": np.ascontiguousarray(x[b, s * SL:(s + 1) * SL, :]),
            "wq": wq, "wk": wk, "wv": wv, "wo": wo,
            "pk": pk, "wsc": wsc_rep, "bias_rep": bias_rep,
            "ident": ident, "maskL": maskL, "pmask": pmask,
        })
    return in_maps


def assemble_output(cfg, results):
    D = cfg["D"]; SL = cfg["SL"]; B = cfg["B"]; NCORE = cfg["NCORE"]
    CPB = NCORE // B
    out = np.empty((B, CPB * SL, D), dtype=np.float32)
    for c in range(NCORE):
        b, s = c // CPB, c % CPB
        out[b, s * SL:(s + 1) * SL, :] = results[c]["out"]
    return out








from concourse.bass_utils import run_bass_kernel_spmd

_CFG = dict(D=1024, SL=2048, B=2, NCORE=8)
_NC_CACHE = [None]


def kernel(L0_attn_x, L0_attn_W_q, L0_attn_W_k, L0_attn_W_v, L0_attn_W_o,
           L0_ffn_packed_weight, L0_ffn_weight_scale, L0_ffn_bias):
    cfg = _CFG
    full = {
        "x": np.asarray(L0_attn_x),
        "Wq": np.asarray(L0_attn_W_q), "Wk": np.asarray(L0_attn_W_k),
        "Wv": np.asarray(L0_attn_W_v), "Wo": np.asarray(L0_attn_W_o),
        "packed": np.asarray(L0_ffn_packed_weight),
        "wscale": np.asarray(L0_ffn_weight_scale),
        "bias": np.asarray(L0_ffn_bias),
    }
    if _NC_CACHE[0] is None:
        _NC_CACHE[0] = build_program(cfg)
    nc = _NC_CACHE[0]
    in_maps = make_inputs(cfg, full)
    res = run_bass_kernel_spmd(nc, in_maps, core_ids=list(range(cfg["NCORE"])))
    return assemble_output(cfg, res.results)


# version 20
# speedup vs baseline: 1.0161x; 1.0161x over previous
import numpy as np
from contextlib import ExitStack
import concourse.bass as bass
import concourse.mybir as mybir

F32 = mybir.dt.float32
BF16 = mybir.dt.bfloat16
I32 = mybir.dt.int32
AF = mybir.ActivationFunctionType
ALU = mybir.AluOpType
AX = mybir.AxisListType

ENGINES = ("sync", "tensor", "vector", "scalar", "gpsimd")


class Sched:
    def __init__(self):
        self.q = {e: [] for e in ENGINES}
        self.cnt = {}
        self.sems = {}

    def setup(self, nc, stack, names):
        for n in names:
            self.sems[n] = stack.enter_context(nc.semaphore(n))
            self.cnt[n] = 0

    def run(self, eng, fn, waits=(), inc=None):
        wl = {}
        for t in waits:
            if t is None:
                continue
            ch, val = t
            if val > 0:
                wl[ch] = max(wl.get(ch, 0), val)
        tok = None
        if inc is not None:
            ch, amt = inc
            self.cnt[ch] += amt
            tok = (ch, self.cnt[ch])
        self.q[eng].append((wl, fn, inc))
        return tok

    def emit_into(self, block):
        sched = self
        waited = {}
        for eng in ENGINES:
            for wl, fn, inc in self.q[eng]:
                for ch, val in wl.items():
                    waited.setdefault(ch, set()).add(val)
        keepsets = {}
        remap = {}
        for ch, vals in waited.items():
            sv = sorted(vals)
            keepsets[ch] = set(sv)
            amt = 16 if (ch.startswith("ld") or ch.startswith("sp") or ch.startswith("lc")) else 1
            remap[ch] = {v: (i + 1) * amt for i, v in enumerate(sv)}

        def body_for(name):
            def body(e):
                cnt = {}
                for wl, fn, inc in sched.q[name]:
                    for ch, val in wl.items():
                        e.wait_ge(sched.sems[ch], remap[ch][val])
                    if fn is not None:
                        inst = fn(e)
                        if inc is not None:
                            ch, amt = inc
                            cnt[ch] = cnt.get(ch, 0) + amt
                            if ch in keepsets and cnt[ch] in keepsets[ch]:
                                inst.then_inc(sched.sems[ch], amt)
            return body

        block.sync(body_for("sync"))
        block.tensor(body_for("tensor"))
        block.vector(body_for("vector"))
        block.scalar(body_for("scalar"))
        block.gpsimd(body_for("gpsimd"))


def build_program(cfg):
    D = cfg["D"]; SL = cfg["SL"]; B = cfg["B"]; NCORE = cfg["NCORE"]
    KC = D // 128
    H = D // 64
    NJ = H // 2
    NT = SL // 128
    CPB = NCORE // B
    R = D // 128
    NPW = D * D // 512
    NH = D // 512
    assert D % 512 == 0 and SL % 512 == 0

    nc = bass.Bass("TRN2", target_bir_lowering=False)

    x_in = nc.dram_tensor("x_in", [SL, D], F32, kind="ExternalInput").ap()
    wq_in = nc.dram_tensor("wq", [D, D], F32, kind="ExternalInput").ap()
    wk_in = nc.dram_tensor("wk", [D, D], F32, kind="ExternalInput").ap()
    wv_in = nc.dram_tensor("wv", [D, D], F32, kind="ExternalInput").ap()
    wo_in = nc.dram_tensor("wo", [D, D], F32, kind="ExternalInput").ap()
    pk_in = nc.dram_tensor("pk", [128, R * D], I32, kind="ExternalInput").ap()
    wsc_in = nc.dram_tensor("wsc", [128, 1], F32, kind="ExternalInput").ap()
    bias_in = nc.dram_tensor("bias_rep", [128, D], F32, kind="ExternalInput").ap()
    ident_in = nc.dram_tensor("ident", [128, 128], F32, kind="ExternalInput").ap()
    maskL_in = nc.dram_tensor("maskL", [128, 128], F32, kind="ExternalInput").ap()
    pm_in = nc.dram_tensor("pmask", [128, CPB], F32, kind="ExternalInput").ap()
    out = nc.dram_tensor("out", [SL, D], F32, kind="ExternalOutput").ap()

    qt_h = nc.dram_tensor("qt_h", [128, NJ, SL], F32).ap()
    kt_h = nc.dram_tensor("kt_h", [128, NJ, SL], F32).ap()
    v_h = nc.dram_tensor("v_h", [SL, D], F32).ap()
    g1_h = nc.dram_tensor("g1_h", [NT, 128, NJ * 128], F32).ap()
    g2_h = nc.dram_tensor("g2_h", [NT, 128, D], F32).ap()
    summ_t = nc.dram_tensor("summ", [3 * 128, NJ * 64], F32)
    summ_all_t = nc.dram_tensor("summ_all", [CPB * 3 * 128, NJ * 64], F32)

    S = Sched()
    stack = ExitStack()
    with stack:
        sbuf = lambda name, shape, dt=F32: stack.enter_context(nc.sbuf_tensor(name, shape, dt))

        ident = sbuf("identt", [128, 128])
        maskL = sbuf("maskl", [128, 128])
        pm = sbuf("pms", [128, CPB])
        wscv = sbuf("wscv", [128, 1])
        bias = sbuf("biasr", [128, D])
        qint = sbuf("qint", [128, max(2 * NPW, 2 * D)], I32)
        enc = qint[:, 0:NPW]
        pk = qint[:, NPW:2 * NPW]
        q32b = qint[:, D:2 * D]
        pv = sbuf("pvs", [128, 1])
        p7 = sbuf("p7s", [128, 1])
        p7i = sbuf("p7i", [128, 1], I32)
        mrs = sbuf("mrs", [128, 1])
        halfv = sbuf("halfv", [128, 1])
        tffn = sbuf("tffn", [128, KC * D], BF16)
        qxtb = sbuf("qxtb", [128, D], BF16)
        WW = NJ * 64
        lnxT = sbuf("lnxT", [128, max(KC * SL, 8 * D + 4 * CPB * WW)])
        wun = lnxT[:, 0:R * D]
        wbig = sbuf("wbig", [128, max(KC * D, 2 * NJ * 128 + 4 * D)])
        ktc2 = [wbig[:, 0:NJ * 128], wbig[:, NJ * 128:2 * NJ * 128]]
        ktok2 = [wbig[:, 2 * NJ * 128:2 * NJ * 128 + D],
                 wbig[:, 2 * NJ * 128 + D:2 * NJ * 128 + 2 * D]]
        qtok2 = [wbig[:, 2 * NJ * 128 + 2 * D:2 * NJ * 128 + 3 * D],
                 wbig[:, 2 * NJ * 128 + 3 * D:2 * NJ * 128 + 4 * D]]
        xt = [wbig[:, 0:D], wbig[:, D:2 * D]]
        stgbig = sbuf("stgbig", [128, 1024])
        stg = [stgbig[:, 0:512], stgbig[:, 512:1024]]
        qtc2 = [sbuf("qtc0", [128, NJ * 128]), sbuf("qtc1", [128, NJ * 128])]
        vc2 = [sbuf("vc0", [128, D]), sbuf("vc1", [128, D])]
        g1c2 = [sbuf("g1c0", [128, NJ * 128]), sbuf("g1c1", [128, NJ * 128])]
        g2c2 = [sbuf("g2c0", [128, D]), sbuf("g2c1", [128, D])]
        otc3 = [sbuf("otc0", [128, NJ * 128]), sbuf("otc1", [128, NJ * 128]),
                sbuf("otc2x", [128, NJ * 128])]
        asm = [sbuf("asm0", [128, 128]), sbuf("asm1", [128, 128]),
               sbuf("asm2", [128, 128]), sbuf("asm3", [128, 128])]
        g1f2 = [sbuf("g1f0", [128, 128]), sbuf("g1f1", [128, 128])]
        g2f2 = [sbuf("g2f0", [128, 128]), sbuf("g2f1", [128, 128])]
        s1run = sbuf("s1run", [128, NJ * 128])
        hloc = sbuf("hloc", [128, NJ * 64])
        crun = sbuf("crun", [128, NJ * 64])
        s1in = sbuf("s1in", [128, NJ * 64])
        hin = sbuf("hin", [128, NJ * 64])
        hbd = sbuf("hbd", [128, NJ * 128])
        s1bd = sbuf("s1bd", [128, NJ * 128])
        pbd = otc3[0][:, 0:NJ * 128]
        tmpst = otc3[1][:, 0:NJ * 64]
        nmv = sbuf("nmv", [128, 1])
        vsv = sbuf("vsv", [128, 1])
        rrv = sbuf("rrv", [128, 1])
        rsv = sbuf("rsv", [128, 1])
        nwa = sbuf("nwa", [128, 1])
        sqd = stgbig[:, 0:min(D, 1024)]
        mxv = sbuf("mxv", [128, 1])
        scv = sbuf("scv", [128, 1])
        epv = sbuf("epv", [128, 1])
        q32 = qint[:, 0:D]
        sa_a = lnxT[:, 7 * D + 0 * CPB * WW: 7 * D + 1 * CPB * WW]
        sa_b = lnxT[:, 7 * D + 1 * CPB * WW: 7 * D + 2 * CPB * WW]
        sa_c = lnxT[:, 7 * D + 2 * CPB * WW: 7 * D + 3 * CPB * WW]
        pfx = lnxT[:, 7 * D + 3 * CPB * WW: 7 * D + 4 * CPB * WW]
        x1 = lnxT[:, 0 * D:1 * D]
        xr_extra = lnxT[:, 7 * D + 4 * CPB * WW: 8 * D + 4 * CPB * WW]
        ln2 = lnxT[:, 1 * D:2 * D]
        abv = lnxT[:, 2 * D:3 * D]
        sgv = lnxT[:, 3 * D:4 * D]
        out2 = lnxT[:, 5 * D:6 * D]
        xr3 = [lnxT[:, 6 * D:7 * D], lnxT[:, 4 * D:5 * D], xr_extra]

        pb = [stack.enter_context(nc.psum_tensor(f"pb{i}", [128, 512], F32)) for i in range(8)]
        bank_readers = [[] for _ in range(8)]
        bank_writer = [None] * 8

        NSLOT = 8
        ld_slots = [f"ld{i}" for i in range(NSLOT)]
        lc_slots = [f"lc{i}" for i in range(4)]
        sp_slots = [f"sp{i}" for i in range(NSLOT)]
        S.setup(nc, stack, ld_slots + lc_slots + sp_slots + ["pe", "dv", "ac", "gp", "cc"])
        dma_idx = {"sync": 0, "gpsimd": 0, "scalar": 0}

        def _dma(eng, slots, dst, src, waits):
            i = dma_idx[eng]
            dma_idx[eng] += 1
            ch = slots[i % len(slots)]
            prev = S.cnt[ch]
            return S.run(eng, lambda e, d=dst, s=src: e.dma_start(out=d, in_=s),
                         list(waits) + [(ch, prev)], (ch, 16))

        def dma_in(dst, src, waits=()):
            return _dma("sync", ld_slots, dst, src, waits)

        def dma_in2(dst, src, waits=()):
            if cfg.get("no_scalar_dma"):
                return _dma("sync", ld_slots, dst, src, waits)
            return _dma("scalar", lc_slots, dst, src, waits)

        def dma_out(dst, src, waits=()):
            return _dma("gpsimd", sp_slots, dst, src, waits)

        def mm(bi, o, l, r, start, stop, tp=None, waits=(), skip=False):
            w = list(waits) + bank_readers[bi]
            bank_readers[bi] = []
            t = S.run("tensor",
                      lambda e, o=o, l=l, r=r, a=start, z=stop, t=tp, k=skip:
                      e.matmul(o, l, r, start=a, stop=z, tile_position=t, skip_group_check=k),
                      w, ("pe", 1))
            bank_writer[bi] = t
            return t

        def tr(bi, o, i, waits=()):
            w = list(waits) + bank_readers[bi]
            bank_readers[bi] = []
            t = S.run("tensor", lambda e, o=o, i=i: e.transpose(o, i, ident[0:i.shape[0], 0:i.shape[0]]),
                      w, ("pe", 1))
            bank_writer[bi] = t
            return t

        def vop(fn, waits=(), reads_bank=None):
            if reads_bank is not None:
                waits = list(waits) + [bank_writer[reads_bank]]
            t = S.run("vector", fn, waits, ("dv", 1))
            if reads_bank is not None:
                bank_readers[reads_bank].append(t)
            return t

        def aop(fn, waits=(), reads_bank=None):
            if reads_bank is not None:
                waits = list(waits) + [bank_writer[reads_bank]]
            t = S.run("scalar", fn, waits, ("ac", 1))
            if reads_bank is not None:
                bank_readers[reads_bank].append(t)
            return t

        def evac(dst, src, bi, waits=(), use="v"):
            if use == "v":
                return vop(lambda e, d=dst, s=src: e.tensor_copy(d, s), waits, reads_bank=bi)
            return aop(lambda e, d=dst, s=src: e.copy(d, s), waits, reads_bank=bi)

        def tr_slot(k):
            bk = k % 2
            bi = 7 if bk else 0
            sl = pb[bi][:, ((k // 2) % 2) * 128:(((k // 2) % 2) + 1) * 128]
            return bi, sl

        t_id = dma_in(ident[:, :], ident_in)
        t_p7 = vop(lambda e: e.memset(p7[:, :], 0.7))
        t_half = vop(lambda e: e.memset(halfv[:, :], 0.5))
        t_p7b = aop(lambda e: e.activation(p7i[:, :], p7[:, :], AF.Identity), [t_p7])
        t_pvp = aop(lambda e: e.activation(pv[:, :], p7i[:, :], AF.Identity), [t_p7b])
        t_mk = dma_in(maskL[:, :], maskL_in)
        t_pm = dma_in(pm[:, :], pm_in)
        t_wsc = dma_in(wscv[:, :], wsc_in)
        t_bias = dma_in(bias[:, :], bias_in)
        t_wf = []
        t_pk_prev = None
        NWCH = R * D // NPW
        for cw in range(NWCH):
            t_pk = dma_in(pk[:, :], pk_in[:, cw * NPW:(cw + 1) * NPW], [t_pk_prev])
            t_es = []
            for f in range(4):
                t_es.append(vop(lambda e, f=f: e.tensor_scalar(enc[32 * f:32 * (f + 1), :],
                                                               pk[32 * f:32 * (f + 1), :], 2 * f, 3,
                                                               ALU.logical_shift_right, ALU.bitwise_and),
                                [t_pk]))
            t_wf.append(vop(lambda e, cw=cw: e.tensor_scalar(wun[:, cw * NPW:(cw + 1) * NPW], enc[:, :],
                                                             1.0, None, ALU.subtract), t_es))
            t_pk_prev = t_wf[-1]
        tffn_toks = []
        si = 0
        for r in range(R):
            for ic in range(KC):
                bi, sl = tr_slot(si)
                t_t = tr(bi, sl, wun[:, r * D + ic * 128: r * D + ic * 128 + 128], [t_id] + t_wf)
                dst = tffn[:, ic * D + r: ic * D + r + R * 127 + 1: R]
                tffn_toks.append(evac(dst, sl, bi, [t_t], use=("v" if si % 2 else "a")))
                si += 1
        t_tffn = [max([t for t in tffn_toks if t[0] == "dv"], key=lambda x: x[1]),
                  max([t for t in tffn_toks if t[0] == "ac"], key=lambda x: x[1])]
        if cfg.get("debug7"):
            dbg7 = nc.dram_tensor("dbg7", [128, 2 * D], BF16, kind="ExternalOutput").ap()
            dma_out(dbg7[:, :], tffn[:, 0:2 * D], t_tffn)

        tile_tok = [None] * NT
        xt_free = [[], []]
        for i in range(NT):
            bi_ = i % 2
            t_x = dma_in(xt[bi_][:, :], x_in[i * 128:(i + 1) * 128, :], xt_free[bi_])
            t_s1 = vop(lambda e, b=bi_: e.tensor_reduce(nmv[:, :], xt[b][:, :], AX.X, ALU.add),
                       [t_x, tile_tok[i - 1][0] if i else None])
            t_nm = vop(lambda e: e.tensor_scalar(nmv[:, :], nmv[:, :], -1.0 / D, None, ALU.mult), [t_s1])
            t_sq = aop(lambda e, b=bi_: e.activation(sqd[:, :], xt[b][:, :], AF.Square, bias=nmv[:, :], scale=1.0,
                                                     accum_out=vsv[:, :]), [t_nm])
            t_v1 = vop(lambda e: e.tensor_scalar(vsv[:, :], vsv[:, :], 1.0 / D, 1e-5, ALU.mult, ALU.add), [t_sq])
            t_rr = vop(lambda e: e.reciprocal(rrv[:, :], vsv[:, :]), [t_v1])
            t_rs = aop(lambda e: e.activation(rsv[:, :], rrv[:, :], AF.Sqrt), [t_rr])
            t_n1 = vop(lambda e: e.tensor_tensor(nwa[:, :], rsv[:, :], rsv[:, :], ALU.mult), [t_rs])
            t_n2 = vop(lambda e: e.tensor_tensor(nwa[:, :], nwa[:, :], vsv[:, :], ALU.mult), [t_n1])
            t_n3 = vop(lambda e: e.tensor_scalar(nwa[:, :], nwa[:, :], -0.5, 1.5, ALU.mult, ALU.add), [t_n2])
            t_n4 = vop(lambda e: e.tensor_tensor(rsv[:, :], rsv[:, :], nwa[:, :], ALU.mult), [t_n3])
            t_ln = vop(lambda e, b=bi_: e.tensor_scalar(xt[b][:, :], xt[b][:, :], nmv[:, :], rsv[:, :],
                                                        ALU.add, ALU.mult), [t_n4])
            toks = []
            for kc in range(KC):
                bi2, sl = tr_slot(kc)
                t_t = tr(bi2, sl, xt[bi_][:, kc * 128:(kc + 1) * 128], [t_ln, t_id])
                dst = lnxT[:, kc * SL + i * 128: kc * SL + (i + 1) * 128]
                toks.append(evac(dst, sl, bi2, [t_t], use=("v" if kc % 2 == 0 else "a")))
            dmax = max([t for t in toks if t[0] == "dv"], key=lambda x: x[1], default=None)
            amax = max([t for t in toks if t[0] == "ac"], key=lambda x: x[1], default=None)
            tile_tok[i] = (dmax, amax)
            xt_free[bi_] = [dmax, amax]

        def tiles_tok(i0, i1):
            o_ = []
            for i in range(i0, i1):
                o_ += [tile_tok[i][0], tile_tok[i][1]]
            return o_

        stg_free = [None, None]
        last_w_mm = None
        t_p1_done = tiles_tok(NT - 1, NT)
        for wsrc, dsth in ((wq_in, qt_h), (wk_in, kt_h)):
            t_w = []
            for kc in range(KC):
                t_w.append(dma_in(wbig[:, kc * D:(kc + 1) * D], wsrc[kc * 128:(kc + 1) * 128, :],
                                  [last_w_mm] + t_p1_done))
            for oc in range(NJ):
                for tg in range(SL // 512):
                    par = tg % 2
                    bi = 2 + par
                    t_m = None
                    for kc in range(KC):
                        t_m = mm(bi, pb[bi][:, :], wbig[:, kc * D + oc * 128: kc * D + oc * 128 + 128],
                                 lnxT[:, kc * SL + tg * 512: kc * SL + tg * 512 + 512],
                                 start=(kc == 0), stop=(kc == KC - 1),
                                 waits=[t_w[kc]] + tiles_tok(tg * 4, tg * 4 + 4))
                    last_w_mm = t_m
                    t_ev = evac(stg[par][:, :], pb[bi][:, :], bi, [t_m, stg_free[par]])
                    t_sp = dma_out(dsth[:, oc, tg * 512:(tg + 1) * 512], stg[par][:, :], [t_ev])
                    stg_free[par] = t_sp

        t_wv = []
        for kc in range(KC):
            t_wv.append(dma_in(wbig[:, kc * D:(kc + 1) * D], wv_in[kc * 128:(kc + 1) * 128, :], [last_w_mm]))
        for i in range(NT):
            for nh in range(NH):
                par = (i * NH + nh) % 2
                bi = 2 + par
                t_m = None
                for kc in range(KC):
                    t_m = mm(bi, pb[bi][:, :], lnxT[:, kc * SL + i * 128: kc * SL + (i + 1) * 128],
                             wbig[:, kc * D + nh * 512: kc * D + nh * 512 + 512],
                             start=(kc == 0), stop=(kc == KC - 1),
                             waits=[t_wv[kc]] + tiles_tok(i, i + 1))
                last_w_mm = t_m
                t_ev = evac(stg[par][:, :], pb[bi][:, :], bi, [t_m, stg_free[par]])
                t_sp = dma_out(v_h[i * 128:(i + 1) * 128, nh * 512:(nh + 1) * 512], stg[par][:, :], [t_ev])
                stg_free[par] = t_sp
        t_spills_l = [stg_free[0], stg_free[1]]

        SA = cfg.get("stop_after", 9)
        if SA < 3:
            S.run("gpsimd", None, waits=[(ch, S.cnt[ch]) for ch in sp_slots])
            with nc.Block() as block:
                S.emit_into(block)
            return nc
        t_z1 = vop(lambda e: e.memset(s1run[:, :], 0.0))
        t_z2 = vop(lambda e: e.memset(hloc[:, :], 0.0))
        t_z3 = vop(lambda e: e.memset(crun[:, :], 0.0))
        s1_tok = t_z1
        hl_tok = t_z2
        cr_tok = t_z3
        u3 = {"qtc": [[], []], "ktc": [[], []], "vc": [[], []],
              "ktok": [[], []], "qtok": [[], []]}
        loads3 = [None] * (NT + 1)

        def p3_load(c):
            p_ = c % 2
            ex = t_spills_l if c < 2 else []
            tq = dma_in(qtc2[p_][:, :], qt_h[:, :, c * 128:(c + 1) * 128], ex + u3["qtc"][p_])
            tk = dma_in(ktc2[p_][:, :], kt_h[:, :, c * 128:(c + 1) * 128], ex + u3["ktc"][p_])
            tv = dma_in2(vc2[p_][:, :], v_h[c * 128:(c + 1) * 128, :], ex + u3["vc"][p_])
            u3["qtc"][p_] = []; u3["ktc"][p_] = []; u3["vc"][p_] = []
            return (tq, tk, tv)

        loads3[0] = p3_load(0)
        g1_sp_par = [None, None]
        g2c_rd = [[], []]
        g1c_rd = [[], []]
        for c in range(NT):
            p = c % 2
            t_qt, t_kt, t_vl = loads3[c]
            if c + 1 < NT:
                loads3[c + 1] = p3_load(c + 1)
            qtc = qtc2[p]; ktc = ktc2[p]; vc = vc2[p]
            ktok = ktok2[p]; qtok = qtok2[p]
            g1c = g1c2[p]; g2c = g2c2[p]
            g1c_rd_old = g1c_rd[p]
            g1c_rd[p] = []
            g2c_rd_old = g2c_rd[p]
            g2c_rd[p] = []
            old_ktok_u = u3["ktok"][p]; u3["ktok"][p] = []
            old_qtok_u = u3["qtok"][p]; u3["qtok"][p] = []
            ktok_toks = [None] * NJ
            qtok_toks = [None] * NJ
            for jj in range(NJ):
                bi, sl = tr_slot(jj)
                t_t = tr(bi, sl, ktc[:, jj * 128:(jj + 1) * 128], [t_kt] + old_ktok_u)
                u3["ktc"][p].append(t_t)
                ktok_toks[jj] = evac(ktok[:, jj * 128:(jj + 1) * 128], sl, bi, [t_t],
                                     use=("v" if jj % 2 else "a"))
            for jj in range(NJ):
                bi, sl = tr_slot(jj + NJ)
                t_t = tr(bi, sl, qtc[:, jj * 128:(jj + 1) * 128], [t_qt] + old_qtok_u)
                u3["qtc"][p].append(t_t)
                qtok_toks[jj] = evac(qtok[:, jj * 128:(jj + 1) * 128], sl, bi, [t_t],
                                     use=("v" if jj % 2 else "a"))

            am_toks = [None] * NJ

            def stage_A(jj):
                jw = slice(jj * 128, (jj + 1) * 128)
                res = []
                for hh, b in ((0, 0), (1, 64)):
                    bs = slice(b, b + 64)
                    bi = 4 - hh
                    t_as = mm(bi, pb[bi][:, 0:128], ktc[bs, jw], qtc[bs, jw], True, True,
                              tp=(b, 0), waits=[t_kt, t_qt])
                    u3["qtc"][p].append(t_as); u3["ktc"][p].append(t_as)
                    ai = (jj % 2) * 2 + hh
                    res.append(vop(lambda e, a=ai, bb=bi: e.tensor_tensor(asm[a][:, :], pb[bb][:, 0:128],
                                                                         maskL[:, :], ALU.mult),
                                   [t_as, t_mk], reads_bank=bi))
                am_toks[jj] = res

            g1_evs, g2_evs = [], []
            corr_toks = []
            dS1_mms = []
            dH_mms = []

            def stage_G(jj):
                jw = slice(jj * 128, (jj + 1) * 128)
                jw64 = slice(jj * 64, (jj + 1) * 64)
                t_am = am_toks[jj]
                for hh, b in ((0, 0), (1, 64)):
                    bs = slice(b, b + 64)
                    hc = slice((2 * jj + hh) * 64, (2 * jj + hh) * 64 + 64)
                    ai = (jj % 2) * 2 + hh
                    t_a = mm(5, pb[5][bs, 0:128], ktok[:, hc], asm[ai][:, :], True, False, tp=(0, b),
                             waits=[ktok_toks[jj], t_am[hh]], skip=True)
                    u3["ktok"][p].append(t_a)
                t_b = mm(5, pb[5][:, 0:128], s1run[:, jw], qtc[:, jw], False, True, tp=(0, 0),
                         waits=[s1_tok], skip=True)
                u3["qtc"][p].append(t_b)
                corr_toks.append(t_b)
                t_g1ev = vop(lambda e, d=g1c[:, jw]: e.tensor_copy(d, pb[5][:, 0:128]),
                             [t_b, g1_sp_par[p]] + g1c_rd_old, reads_bank=5)
                g1_evs.append(t_g1ev)
                bi6, sl6 = tr_slot(2 * jj + (c % 2))
                t_t2 = tr(bi6, sl6, g1c[:, jw], [t_g1ev])
                g1c_rd[p].append(t_t2)
                t_g2ev = evac(g2c[:, jw], sl6, bi6, [t_t2] + g2c_rd_old, use="a")
                g2_evs.append(t_g2ev)
                for hh, b in ((0, 0), (1, 64)):
                    hc = slice((2 * jj + hh) * 64, (2 * jj + hh) * 64 + 64)
                    t_d = mm(1, pb[1][slice(b, b + 64), jw64], ktok[:, hc], ktok[:, hc], True, True,
                             tp=(0, b), waits=[ktok_toks[jj]])
                    u3["ktok"][p].append(t_d)
                    dS1_mms.append(t_d)
                for hh, b in ((0, 0), (1, 64)):
                    hc = slice((2 * jj + hh) * 64, (2 * jj + hh) * 64 + 64)
                    t_d = mm(2, pb[2][slice(b, b + 64), jw64], g2c[:, hc], vc[:, hc], True, True,
                             tp=(0, b), waits=[t_g2ev, t_vl])
                    u3["vc"][p].append(t_d)
                    g2c_rd[p].append(t_d)
                    dH_mms.append(t_d)

            stage_A(0)
            for jj in range(NJ):
                if jj + 1 < NJ:
                    stage_A(jj + 1)
                stage_G(jj)

            s1e_o = s1run[0:64, :].rearrange("p (j c) -> p j c", c=128)[:, :, 0:64]
            s1e_i = pb[1][0:64, 0:NJ * 64].rearrange("p (j c) -> p j c", c=64)
            s1o_o = s1run[64:128, :].rearrange("p (j c) -> p j c", c=128)[:, :, 64:128]
            s1o_i = pb[1][64:128, 0:NJ * 64].rearrange("p (j c) -> p j c", c=64)
            sa1a = vop(lambda e: e.tensor_tensor(s1e_o, s1e_i, s1e_o, ALU.add),
                       [s1_tok] + dS1_mms + corr_toks, reads_bank=1)
            sa1 = vop(lambda e: e.tensor_tensor(s1o_o, s1o_i, s1o_o, ALU.add),
                      [sa1a], reads_bank=1)
            s1_tok = sa1
            ha1 = vop(lambda e: e.tensor_tensor(hloc[:, :], pb[2][:, 0:NJ * 64], hloc[:, :], ALU.add),
                      [hl_tok] + dH_mms, reads_bank=2)
            hl_tok = ha1
            dC_mms = []
            for h in range(H):
                jj, hh = h // 2, h % 2
                hc = slice(h * 64, h * 64 + 64)
                t_d = mm(1, pb[1][slice(64 * hh, 64 * hh + 64), jj * 64:(jj + 1) * 64],
                         qtok[:, hc], vc[:, hc], True, True, tp=(0, 64 * hh),
                         waits=[qtok_toks[jj], t_vl])
                u3["qtok"][p].append(t_d)
                u3["vc"][p].append(t_d)
                dC_mms.append(t_d)
            ca1 = vop(lambda e: e.tensor_tensor(crun[:, :], pb[1][:, 0:NJ * 64], crun[:, :], ALU.add),
                      [cr_tok] + dC_mms, reads_bank=1)
            cr_tok = ca1
            g1_sp_par[p] = dma_out(g1_h[c, :, :], g1c[:, :], g1_evs)

        if SA < 4:
            S.run("gpsimd", None, waits=[(ch, S.cnt[ch]) for ch in sp_slots])
            with nc.Block() as block:
                S.emit_into(block)
            return nc
        u5 = {"qtc": [[], []], "vc": [[], []], "g1": [[], []], "g2": [[], []], "otc": [[], [], []]}
        u5["qtc"][0] = [t for t in u3["qtc"][0] if t is not None]
        u5["qtc"][1] = [t for t in u3["qtc"][1] if t is not None]
        u5["vc"][0] = u3["vc"][0]; u5["vc"][1] = u3["vc"][1]
        u5["g1"][0] = [g1_sp_par[0]] + g1c_rd[0]
        u5["g1"][1] = [g1_sp_par[1]] + g1c_rd[1]

        loads5 = [None] * (NT + 1)

        def p5_load(c):
            p_ = c % 2
            tq = dma_in(qtc2[p_][:, :], qt_h[:, :, c * 128:(c + 1) * 128], u5["qtc"][p_])
            tv = dma_in2(vc2[p_][:, :], v_h[c * 128:(c + 1) * 128, :], u5["vc"][p_])
            tg1 = dma_in(g1c2[p_][:, :], g1_h[c, :, :], u5["g1"][p_])
            u5["qtc"][p_] = []; u5["vc"][p_] = []; u5["g1"][p_] = []
            return (tq, tv, tg1, None)

        loads5[0] = p5_load(0)
        if NT > 1:
            loads5[1] = p5_load(1)

        tp_e = tmpst[0:64, :].rearrange("p (j c) -> p j c", c=64)
        tp_o = tmpst[64:128, :].rearrange("p (j c) -> p j c", c=64)
        s1bde = s1run[0:64, :].rearrange("p (j c) -> p j c", c=128)[:, :, 0:64]
        s1bdo = s1run[64:128, :].rearrange("p (j c) -> p j c", c=128)[:, :, 64:128]
        t_x1e = vop(lambda e: e.tensor_copy(tp_e, s1bde), [s1_tok])
        t_x1o = vop(lambda e: e.tensor_copy(tp_o, s1bdo), [t_x1e])
        t_sm0 = dma_out(summ_t.ap()[0:128, :], tmpst[:, :], [t_x1o])
        t_sm1 = dma_out(summ_t.ap()[128:256, :], hloc[:, :], [hl_tok])
        t_sm2 = dma_out(summ_t.ap()[256:384, :], crun[:, :], [cr_tok])
        groups = [list(range(g * CPB, (g + 1) * CPB)) for g in range(B)]
        t_cc = S.run("gpsimd",
                     lambda e: e.collective_compute("AllGather", ALU.bypass, replica_groups=groups,
                                                    ins=[summ_t.ap().opt()], outs=[summ_all_t.ap().opt()]),
                     [t_sm0, t_sm1, t_sm2], ("cc", 1))
        sall3 = summ_all_t.ap().rearrange("(i k p) w -> i k p w", k=3, p=128)
        t_la = []
        for kind, dst in ((0, sa_a), (1, sa_b), (2, sa_c)):
            t_la.append(dma_in(dst.rearrange("p (i w) -> p i w", i=CPB),
                               sall3[:, kind, :, :].rearrange("i p w -> p i w"), [t_cc]))
        t_pz = vop(lambda e: e.memset(pfx[:, 0:WW], 0.0))
        t_pv = [t_pz]
        for i in range(1, CPB):
            t_pv.append(vop(lambda e, i=i: e.tensor_tensor(pfx[:, i * WW:(i + 1) * WW], pfx[:, (i - 1) * WW:i * WW],
                                                           sa_a[:, (i - 1) * WW:i * WW], ALU.add), [t_pv[-1]] + t_la))
        t_s1z = vop(lambda e: e.memset(s1in[:, :], 0.0))
        t_acc = t_s1z
        for i in range(CPB):
            t_t1 = vop(lambda e, i=i: e.tensor_scalar(tmpst[:, :], sa_a[:, i * WW:(i + 1) * WW], pm[:, i:i + 1], None,
                                                      ALU.mult), t_la + [t_pm, t_acc, t_sm0])
            t_acc = vop(lambda e: e.tensor_tensor(s1in[:, :], s1in[:, :], tmpst[:, :], ALU.add), [t_t1, t_acc])
        t_s1in = t_acc
        t_bz = vop(lambda e: e.memset(s1bd[:, :], 0.0))
        t_bd = [t_bz]
        for jj in range(NJ):
            t_bd.append(vop(lambda e, jj=jj: e.tensor_copy(s1bd[0:64, jj * 128:jj * 128 + 64],
                                                           s1in[0:64, jj * 64:(jj + 1) * 64]), [t_s1in, t_bz]))
            t_bd.append(vop(lambda e, jj=jj: e.tensor_copy(s1bd[64:128, jj * 128 + 64:jj * 128 + 128],
                                                           s1in[64:128, jj * 64:(jj + 1) * 64]), [t_s1in, t_bz]))
        t_s1bd = [t_bd[-1]]
        t_hz = vop(lambda e: e.memset(hin[:, :], 0.0))
        t_hacc = t_hz
        prev_pbd_users = []
        for i in range(CPB):
            if i == 0:
                t_t1 = vop(lambda e: e.tensor_scalar(tmpst[:, :], sa_b[:, 0:WW], pm[:, 0:1], None, ALU.mult),
                           t_la + [t_pm, t_acc, t_hacc])
                t_hacc = vop(lambda e: e.tensor_tensor(hin[:, :], hin[:, :], tmpst[:, :], ALU.add), [t_t1, t_hacc])
                continue
            t_pbz = vop(lambda e: e.memset(pbd[:, :], 0.0), prev_pbd_users)
            t_pb = [t_pbz]
            for jj in range(NJ):
                t_pb.append(vop(lambda e, jj=jj, i=i: e.tensor_copy(pbd[0:64, jj * 128:jj * 128 + 64],
                                                                   pfx[0:64, i * WW + jj * 64: i * WW + (jj + 1) * 64]),
                                [t_pv[i], t_pbz]))
                t_pb.append(vop(lambda e, jj=jj, i=i: e.tensor_copy(pbd[64:128, jj * 128 + 64:jj * 128 + 128],
                                                                   pfx[64:128, i * WW + jj * 64: i * WW + (jj + 1) * 64]),
                                [t_pv[i], t_pbz]))
            t_x = None
            prev_pbd_users = []
            for jj in range(NJ):
                t_x = mm(2, pb[2][:, jj * 64:(jj + 1) * 64], pbd[:, jj * 128:(jj + 1) * 128],
                         sa_c[:, i * WW + jj * 64: i * WW + (jj + 1) * 64], True, True,
                         waits=t_pb + t_la)
                prev_pbd_users.append(t_x)
            t_t1 = vop(lambda e, i=i: e.tensor_tensor(tmpst[:, :], sa_b[:, i * WW:(i + 1) * WW], pb[2][:, 0:WW], ALU.add),
                       [t_x, t_hacc], reads_bank=2)
            t_t2 = vop(lambda e, i=i: e.tensor_scalar(tmpst[:, :], tmpst[:, :], pm[:, i:i + 1], None, ALU.mult), [t_t1])
            t_hacc = vop(lambda e: e.tensor_tensor(hin[:, :], hin[:, :], tmpst[:, :], ALU.add), [t_t2])
        t_hbz = vop(lambda e: e.memset(hbd[:, :], 0.0), [t_hacc] if False else [])
        t_hb = [t_hbz]
        for jj in range(NJ):
            t_hb.append(vop(lambda e, jj=jj: e.tensor_copy(hbd[0:64, jj * 128:jj * 128 + 64],
                                                           hin[0:64, jj * 64:(jj + 1) * 64]), [t_hacc, t_hbz]))
            t_hb.append(vop(lambda e, jj=jj: e.tensor_copy(hbd[64:128, jj * 128 + 64:jj * 128 + 128],
                                                           hin[64:128, jj * 64:(jj + 1) * 64]), [t_hacc, t_hbz]))
        t_hin = t_hb[-1]

        if cfg.get("debug"):
            dbg = nc.dram_tensor("dbg", [5 * 128, NJ * 64], F32, kind="ExternalOutput").ap()
            dma_out(dbg[0 * 128:1 * 128, :], s1run[:, :], [s1_tok])
            dma_out(dbg[1 * 128:2 * 128, :], hloc[:, :], [hl_tok])
            dma_out(dbg[2 * 128:3 * 128, :], crun[:, :], [cr_tok])
            dma_out(dbg[3 * 128:4 * 128, :], s1in[:, :], [t_s1in])
            t_hin = dma_out(dbg[4 * 128:5 * 128, :], hin[:, :], [t_hin])

        t_wo = []
        for jj in range(NJ):
            t_wo.append(dma_in(wbig[:, jj * D:(jj + 1) * D], wo_in[jj * 128:(jj + 1) * 128, :],
                               [last_w_mm, cr_tok, s1_tok, hl_tok]))

        if SA < 5:
            S.run("gpsimd", None, waits=[(ch, S.cnt[ch]) for ch in sp_slots])
            with nc.Block() as block:
                S.emit_into(block)
            return nc
        hr_tok = t_hin
        t_xr_tok = [None] * NT
        xr_free = [list(t_spills_l), list(t_spills_l), list(t_spills_l)]
        out_free = [None] * NH
        P5N = cfg.get("p5_chunks", NT)
        epi_stageq = []

        def epi_step():
            if epi_stageq:
                epi_stageq.pop(0)()
        for c in range(NT + 1):
            if c < P5N:
                p = c % 2
                t_qt, t_vl, t_g1, t_g2 = loads5[c]
                if c + 1 < P5N and loads5[c + 1] is None:
                    loads5[c + 1] = p5_load(c + 1)
                qtc = qtc2[p]; vc = vc2[p]; g1c = g1c2[p]; g2c = g2c2[p]
                p3_ = c % 3
                otc = otc3[p3_]
                t_xr = dma_in2(xr3[p3_], x_in[c * 128:(c + 1) * 128, :], xr_free[p3_])
                t_xr_tok[c] = t_xr

                gf_toks = [None] * NJ
                ot_evs = []
                dH_mms5 = []
                m6b_toks = []

                def stage_MA(jj):
                    jw = slice(jj * 128, (jj + 1) * 128)
                    bi = 2 + (jj % 2)
                    t_qs = mm(bi, pb[bi][:, 0:128], s1bd[:, jw], qtc[:, jw], True, True,
                              waits=t_s1bd + [t_qt])
                    u5["qtc"][p].append(t_qs)
                    gp = jj % 2
                    t_g1f = vop(lambda e, w=jw, g=gp, bb=bi, src=g1c: e.tensor_tensor(g1f2[g][:, :], pb[bb][:, 0:128],
                                                                            src[:, w], ALU.add),
                                [t_qs, t_g1] + ([gf_toks[jj - 2][2]] if jj >= 2 else []), reads_bank=bi)
                    u5["g1"][p].append(t_g1f)
                    bi7, sl7 = tr_slot(jj)
                    t_tg = tr(bi7, sl7, g1f2[gp][:, :], [t_g1f])
                    t_g2f = evac(g2f2[gp][:, :], sl7, bi7,
                                 [t_tg] + ([gf_toks[jj - 2][3]] if jj >= 2 else []), use="a")
                    gf_toks[jj] = [t_g1f, t_g2f, None, None]

                def stage_MB(jj):
                    jw = slice(jj * 128, (jj + 1) * 128)
                    jw64 = slice(jj * 64, (jj + 1) * 64)
                    gp = jj % 2
                    t_g1f, t_g2f = gf_toks[jj][0], gf_toks[jj][1]
                    t_bm = [None, None]
                    for hh, b in ((0, 0), (1, 64)):
                        bs = slice(b, b + 64)
                        bsb = 4 + hh
                        t_bs = mm(bsb, pb[bsb][:, 0:128], g1f2[gp][bs, :], qtc[bs, jw], True, True,
                                  tp=(b, 0), waits=[t_g1f])
                        u5["qtc"][p].append(t_bs)
                        ai = (jj % 2) * 2 + hh
                        t_bm[hh] = vop(lambda e, a=ai, bb=bsb: e.tensor_tensor(asm[a][:, :], pb[bb][:, 0:128],
                                                                              maskL[:, :], ALU.mult),
                                       [t_bs, t_mk], reads_bank=bsb)
                    for hh, b in ((0, 0), (1, 64)):
                        bs = slice(b, b + 64)
                        hc = slice((2 * jj + hh) * 64, (2 * jj + hh) * 64 + 64)
                        ai = (jj % 2) * 2 + hh
                        t_oa = mm(6, pb[6][bs, 0:128], vc[:, hc], asm[ai][:, :], True, False, tp=(0, b),
                                  waits=[t_vl, t_bm[hh]], skip=True)
                        u5["vc"][p].append(t_oa)
                    t_last = mm(6, pb[6][:, 0:128], hbd[:, jw], qtc[:, jw], False, True, tp=(0, 0),
                                waits=[hr_tok], skip=True)
                    u5["qtc"][p].append(t_last)
                    m6b_toks.append(t_last)
                    for hh, b in ((0, 0), (1, 64)):
                        hc = slice((2 * jj + hh) * 64, (2 * jj + hh) * 64 + 64)
                        t_d = mm(1, pb[1][slice(b, b + 64), jw64], g2f2[gp][:, hh * 64:hh * 64 + 64],
                                 vc[:, hc], True, True, tp=(0, b), waits=[t_g2f, t_vl])
                        u5["vc"][p].append(t_d)
                        dH_mms5.append(t_d)
                        gf_toks[jj][3] = t_d
                    t_oe = evac(otc[:, jw], pb[6][:, 0:128], 6,
                                [t_last] + u5["otc"][p3_], use=("v" if jj % 2 else "a"))
                    if jj == 0:
                        u5["otc"][p3_] = []
                    gf_toks[jj][2] = t_oe
                    ot_evs.append(t_oe)

                if cfg.get("p5_only_loads"):
                    epi_pending = None
                    continue
                stage_MA(0)
                if NJ > 1:
                    stage_MA(1)
                if cfg.get("p5_ma_only"):
                    for jj in range(2, NJ):
                        stage_MA(jj)
                    epi_pending = None
                    continue
                for jj in range(NJ):
                    stage_MB(jj)
                    if jj + 2 < NJ:
                        stage_MA(jj + 2)
                    epi_step()
                    if jj == 0:
                        epi_step()
                        epi_step()
                hb_e = hbd[0:64, :].rearrange("p (j c) -> p j c", c=128)[:, :, 0:64]
                hb_ei = pb[1][0:64, 0:NJ * 64].rearrange("p (j c) -> p j c", c=64)
                hb_o = hbd[64:128, :].rearrange("p (j c) -> p j c", c=128)[:, :, 64:128]
                hb_oi = pb[1][64:128, 0:NJ * 64].rearrange("p (j c) -> p j c", c=64)
                hr_a = vop(lambda e: e.tensor_tensor(hb_e, hb_ei, hb_e, ALU.add),
                           [hr_tok] + dH_mms5 + m6b_toks, reads_bank=1)
                hr_tok = vop(lambda e: e.tensor_tensor(hb_o, hb_oi, hb_o, ALU.add),
                             [hr_a], reads_bank=1)
                epi_step()
                epi_step()

                def make_epi(c=c, p=p3_, ot_evs=tuple(ot_evs), t_xr=t_xr):
                    otc_ = otc3[p]
                    xr_ = xr3[p]
                    st = {}

                    def wo_nh(nh):
                        def f():
                            t_m = None
                            for jj in range(NJ):
                                t_m = mm(7, pb[7][:, :], otc_[:, jj * 128:(jj + 1) * 128],
                                         wbig[:, jj * D + nh * 512: jj * D + nh * 512 + 512],
                                         start=(jj == 0), stop=(jj == NJ - 1),
                                         waits=[t_wo[jj]] + list(ot_evs))
                                u5["otc"][p].append(t_m)
                            t_x1 = vop(lambda e, n=nh, xv=xr_: e.tensor_tensor(x1[:, n * 512:(n + 1) * 512], pb[7][:, :],
                                                                       xv[:, n * 512:(n + 1) * 512], ALU.add),
                                       [t_m, t_xr, out_free[nh]], reads_bank=7)
                            st.setdefault("x1ev", []).append(t_x1)
                        return f

                    def lnq_p1():
                        x1ev = st["x1ev"]
                        t_s1 = aop(lambda e: e.activation(sqd[:, :], x1, AF.Identity, accum_out=nmv[:, :]), x1ev)
                        st["nm"] = vop(lambda e: e.tensor_scalar(nmv[:, :], nmv[:, :], -1.0 / D, None, ALU.mult), [t_s1])

                    def lnq_p2():
                        t_sq = aop(lambda e: e.activation(sqd[:, :], x1, AF.Square, bias=nmv[:, :], scale=1.0,
                                                          accum_out=vsv[:, :]), [st["nm"]])
                        t_v1 = vop(lambda e: e.tensor_scalar(vsv[:, :], vsv[:, :], 1.0 / D, 1e-5, ALU.mult, ALU.add), [t_sq])
                        t_rr = vop(lambda e: e.reciprocal(rrv[:, :], vsv[:, :]), [t_v1])
                        st["rs"] = aop(lambda e: e.activation(rsv[:, :], rrv[:, :], AF.Sqrt), [t_rr])

                    def lnq_p3():
                        t_n1 = vop(lambda e: e.tensor_tensor(nwa[:, :], rsv[:, :], rsv[:, :], ALU.mult), [st["rs"]])
                        t_n2 = vop(lambda e: e.tensor_tensor(nwa[:, :], nwa[:, :], vsv[:, :], ALU.mult), [t_n1])
                        t_n3 = vop(lambda e: e.tensor_scalar(nwa[:, :], nwa[:, :], -0.5, 1.5, ALU.mult, ALU.add), [t_n2])
                        t_n4 = vop(lambda e: e.tensor_tensor(rsv[:, :], rsv[:, :], nwa[:, :], ALU.mult), [t_n3])
                        st["ms"] = vop(lambda e: e.tensor_tensor(mrs[:, :], nmv[:, :], rsv[:, :], ALU.mult), [t_n4])

                    def lnq_p4():
                        t_ln = aop(lambda e: e.activation(ln2, x1, AF.Identity, scale=rsv[:, :], bias=mrs[:, :]), [st["ms"]])
                        st["mx"] = vop(lambda e: e.tensor_reduce(mxv[:, :], ln2, AX.X, ALU.max, apply_absolute_value=True), [t_ln])

                    def lnq_p5():
                        t_mx2 = vop(lambda e: e.tensor_scalar(mxv[:, :], mxv[:, :], 1e-5, None, ALU.max), [st["mx"]])
                        t_sc = vop(lambda e: e.reciprocal(scv[:, :], mxv[:, :]), [t_mx2])
                        t_sc2 = vop(lambda e: e.tensor_scalar(scv[:, :], scv[:, :], 127.0, None, ALU.mult), [t_sc])
                        st["y"] = aop(lambda e: e.activation(abv, ln2, AF.Identity, scale=scv[:, :]), [t_sc2])
                        t_e1 = vop(lambda e: e.tensor_tensor(epv[:, :], mxv[:, :], wscv[:, :], ALU.mult), [t_mx2, t_wsc])
                        st["e2"] = vop(lambda e: e.tensor_scalar(epv[:, :], epv[:, :], 1.0 / 127.0, None, ALU.mult), [t_e1])

                    def lnq_p6():
                        t_r1 = aop(lambda e: e.activation(q32[:, :], abv, AF.Identity), [st["y"]])
                        st["r2"] = aop(lambda e: e.activation(q32b[:, :], abv, AF.Identity, bias=halfv[:, :]),
                                       [t_r1, t_half])

                    def lnq_p7():
                        t_b1 = vop(lambda e: e.tensor_tensor(sgv, q32[:, :], q32b[:, :], ALU.subtract), [st["r2"]])
                        t_b2 = vop(lambda e: e.tensor_scalar(sgv, sgv, pv[:, :], None, ALU.mult), [t_b1, t_pvp])
                        st["qf"] = vop(lambda e: e.tensor_tensor(abv, q32b[:, :], sgv, ALU.add), [t_b2])

                    def lnq_p8():
                        tq_toks = []
                        for kc in range(KC):
                            sl = pb[0][:, (kc % 4) * 128:((kc % 4) + 1) * 128]
                            t_t = tr(0, sl, abv[:, kc * 128:(kc + 1) * 128], [st["qf"]])
                            tq_toks.append(evac(qxtb[:, kc * 128:(kc + 1) * 128], sl, 0, [t_t],
                                                use=("v" if kc % 2 else "a")))
                        st["tq"] = tq_toks

                    def ffn_nh(nh):
                        def f():
                            t_m = None
                            for kc in range(KC):
                                t_m = mm(7, pb[7][:, :], qxtb[:, kc * 128:(kc + 1) * 128],
                                         tffn[:, kc * D + nh * 512: kc * D + nh * 512 + 512],
                                         start=(kc == 0), stop=(kc == KC - 1),
                                         waits=t_tffn + st["tq"])
                            t_f1 = vop(lambda e, n=nh: e.tensor_scalar(out2[:, n * 512:(n + 1) * 512], pb[7][:, :],
                                                                       epv[:, :], None, ALU.mult),
                                       [t_m, st["e2"], out_free[nh]], reads_bank=7)
                            t_f2 = vop(lambda e, n=nh: e.tensor_tensor(out2[:, n * 512:(n + 1) * 512],
                                                                       out2[:, n * 512:(n + 1) * 512],
                                                                       bias[:, n * 512:(n + 1) * 512], ALU.add),
                                       [t_f1, t_bias])
                            t_f3 = vop(lambda e, n=nh: e.tensor_tensor(out2[:, n * 512:(n + 1) * 512],
                                                                       out2[:, n * 512:(n + 1) * 512],
                                                                       x1[:, n * 512:(n + 1) * 512], ALU.add), [t_f2])
                            t_st = dma_out(out[c * 128:(c + 1) * 128, nh * 512:(nh + 1) * 512],
                                           out2[:, nh * 512:(nh + 1) * 512], [t_f3])
                            out_free[nh] = t_st
                            st.setdefault("sts", []).append(t_st)
                            if nh == NH - 1:
                                xr_free[p] = [t_st]
                            if cfg.get("debug5") and c == 0 and nh == NH - 1:
                                dbg5 = nc.dram_tensor("dbg5", [3 * 128, D], F32, kind="ExternalOutput").ap()
                                dma_out(dbg5[0:128, :], x1, [t_st])
                                dma_out(dbg5[128:256, :], out2[:, :], [t_st])
                                t_o5 = vop(lambda e: e.tensor_copy(sgv, qxtb[:, :]), [t_st])
                                dma_out(dbg5[256:384, :], sgv, [t_o5])
                                dbg6 = nc.dram_tensor("dbg6", [128, 2 * D], F32, kind="ExternalOutput").ap()
                                t_o6 = vop(lambda e: e.tensor_copy(x1, tffn[:, 0:D]), [t_st, t_o5])
                                dma_out(dbg6[:, 0:D], x1, [t_o6])
                                t_o7 = vop(lambda e: e.tensor_copy(x1, tffn[:, D:2 * D]), [t_o6])
                                dma_out(dbg6[:, D:2 * D], x1, [t_o7])
                        return f

                    stages = [wo_nh(0)]
                    for nh in range(1, NH):
                        stages.append(wo_nh(nh))
                    stages += [lnq_p1, lnq_p2, lnq_p3, lnq_p4, lnq_p5, lnq_p6, lnq_p7, lnq_p8]
                    for nh in range(NH):
                        stages.append(ffn_nh(nh))
                    return stages

                if not cfg.get("p5_no_epi"):
                    epi_stageq.extend(make_epi())
                if c == P5N - 1:
                    while epi_stageq:
                        epi_step()

        S.run("gpsimd", None, waits=[(ch, S.cnt[ch]) for ch in sp_slots])

        with nc.Block() as block:
            S.emit_into(block)

    return nc


def make_inputs(cfg, full):
    D = cfg["D"]; SL = cfg["SL"]; B = cfg["B"]; NCORE = cfg["NCORE"]
    CPB = NCORE // B
    x = np.ascontiguousarray(full["x"], dtype=np.float32)
    ident = np.eye(128, dtype=np.float32)
    maskL = (np.arange(128)[:, None] <= np.arange(128)[None, :]).astype(np.float32)
    bias_rep = np.ascontiguousarray(np.broadcast_to(full["bias"].astype(np.float32).reshape(1, D), (128, D)))
    wsc_rep = np.full((128, 1), np.float32(np.asarray(full["wscale"]).reshape(-1)[0]), dtype=np.float32)
    RD = D * D // 128
    pkw = np.asarray(full["packed"]).astype(np.int32).reshape(32, RD)
    pk = np.ascontiguousarray(pkw[np.arange(128) % 32])
    wq = np.ascontiguousarray(full["Wq"], dtype=np.float32)
    wk = np.ascontiguousarray(full["Wk"], dtype=np.float32)
    wv = np.ascontiguousarray(full["Wv"], dtype=np.float32)
    wo = np.ascontiguousarray(full["Wo"], dtype=np.float32)
    in_maps = []
    for c in range(NCORE):
        b, s = c // CPB, c % CPB
        pmask = np.zeros((128, CPB), dtype=np.float32)
        pmask[:, :s] = 1.0
        in_maps.append({
            "x_in": np.ascontiguousarray(x[b, s * SL:(s + 1) * SL, :]),
            "wq": wq, "wk": wk, "wv": wv, "wo": wo,
            "pk": pk, "wsc": wsc_rep, "bias_rep": bias_rep,
            "ident": ident, "maskL": maskL, "pmask": pmask,
        })
    return in_maps


def assemble_output(cfg, results):
    D = cfg["D"]; SL = cfg["SL"]; B = cfg["B"]; NCORE = cfg["NCORE"]
    CPB = NCORE // B
    out = np.empty((B, CPB * SL, D), dtype=np.float32)
    for c in range(NCORE):
        b, s = c // CPB, c % CPB
        out[b, s * SL:(s + 1) * SL, :] = results[c]["out"]
    return out








from concourse.bass_utils import run_bass_kernel_spmd

_CFG = dict(D=1024, SL=2048, B=2, NCORE=8)
_NC_CACHE = [None]


def kernel(L0_attn_x, L0_attn_W_q, L0_attn_W_k, L0_attn_W_v, L0_attn_W_o,
           L0_ffn_packed_weight, L0_ffn_weight_scale, L0_ffn_bias):
    cfg = _CFG
    full = {
        "x": np.asarray(L0_attn_x),
        "Wq": np.asarray(L0_attn_W_q), "Wk": np.asarray(L0_attn_W_k),
        "Wv": np.asarray(L0_attn_W_v), "Wo": np.asarray(L0_attn_W_o),
        "packed": np.asarray(L0_ffn_packed_weight),
        "wscale": np.asarray(L0_ffn_weight_scale),
        "bias": np.asarray(L0_ffn_bias),
    }
    if _NC_CACHE[0] is None:
        _NC_CACHE[0] = build_program(cfg)
    nc = _NC_CACHE[0]
    in_maps = make_inputs(cfg, full)
    res = run_bass_kernel_spmd(nc, in_maps, core_ids=list(range(cfg["NCORE"])))
    return assemble_output(cfg, res.results)


# version 21 (final)
# speedup vs baseline: 1.0409x; 1.0243x over previous
import numpy as np
from contextlib import ExitStack
import concourse.bass as bass
import concourse.mybir as mybir

F32 = mybir.dt.float32
BF16 = mybir.dt.bfloat16
I32 = mybir.dt.int32
AF = mybir.ActivationFunctionType
ALU = mybir.AluOpType
AX = mybir.AxisListType

ENGINES = ("sync", "tensor", "vector", "scalar", "gpsimd")


class Sched:
    def __init__(self):
        self.q = {e: [] for e in ENGINES}
        self.cnt = {}
        self.sems = {}

    def setup(self, nc, stack, names):
        for n in names:
            self.sems[n] = stack.enter_context(nc.semaphore(n))
            self.cnt[n] = 0

    def run(self, eng, fn, waits=(), inc=None):
        wl = {}
        for t in waits:
            if t is None:
                continue
            ch, val = t
            if val > 0:
                wl[ch] = max(wl.get(ch, 0), val)
        tok = None
        if inc is not None:
            ch, amt = inc
            self.cnt[ch] += amt
            tok = (ch, self.cnt[ch])
        self.q[eng].append((wl, fn, inc))
        return tok

    def emit_into(self, block):
        sched = self
        waited = {}
        for eng in ENGINES:
            for wl, fn, inc in self.q[eng]:
                for ch, val in wl.items():
                    waited.setdefault(ch, set()).add(val)
        keepsets = {}
        remap = {}
        for ch, vals in waited.items():
            sv = sorted(vals)
            keepsets[ch] = set(sv)
            amt = 16 if (ch.startswith("ld") or ch.startswith("sp") or ch.startswith("lc")) else 1
            remap[ch] = {v: (i + 1) * amt for i, v in enumerate(sv)}

        def body_for(name):
            def body(e):
                cnt = {}
                for wl, fn, inc in sched.q[name]:
                    for ch, val in wl.items():
                        e.wait_ge(sched.sems[ch], remap[ch][val])
                    if fn is not None:
                        inst = fn(e)
                        if inc is not None:
                            ch, amt = inc
                            cnt[ch] = cnt.get(ch, 0) + amt
                            if ch in keepsets and cnt[ch] in keepsets[ch]:
                                inst.then_inc(sched.sems[ch], amt)
            return body

        block.sync(body_for("sync"))
        block.tensor(body_for("tensor"))
        block.vector(body_for("vector"))
        block.scalar(body_for("scalar"))
        block.gpsimd(body_for("gpsimd"))


def build_program(cfg):
    D = cfg["D"]; SL = cfg["SL"]; B = cfg["B"]; NCORE = cfg["NCORE"]
    KC = D // 128
    H = D // 64
    NJ = H // 2
    NT = SL // 128
    CPB = NCORE // B
    R = D // 128
    NPW = D * D // 512
    NH = D // 512
    assert D % 512 == 0 and SL % 512 == 0

    nc = bass.Bass("TRN2", target_bir_lowering=False)

    x_in = nc.dram_tensor("x_in", [SL, D], F32, kind="ExternalInput").ap()
    wq_in = nc.dram_tensor("wq", [D, D], F32, kind="ExternalInput").ap()
    wk_in = nc.dram_tensor("wk", [D, D], F32, kind="ExternalInput").ap()
    wv_in = nc.dram_tensor("wv", [D, D], F32, kind="ExternalInput").ap()
    wo_in = nc.dram_tensor("wo", [D, D], F32, kind="ExternalInput").ap()
    pk_in = nc.dram_tensor("pk", [128, R * D], I32, kind="ExternalInput").ap()
    wsc_in = nc.dram_tensor("wsc", [128, 1], F32, kind="ExternalInput").ap()
    bias_in = nc.dram_tensor("bias_rep", [128, D], F32, kind="ExternalInput").ap()
    ident_in = nc.dram_tensor("ident", [128, 128], F32, kind="ExternalInput").ap()
    maskL_in = nc.dram_tensor("maskL", [128, 128], F32, kind="ExternalInput").ap()
    pm_in = nc.dram_tensor("pmask", [128, CPB], F32, kind="ExternalInput").ap()
    out = nc.dram_tensor("out", [SL, D], F32, kind="ExternalOutput").ap()

    qt_h = nc.dram_tensor("qt_h", [128, NJ, SL], F32).ap()
    kt_h = nc.dram_tensor("kt_h", [128, NJ, SL], F32).ap()
    v_h = nc.dram_tensor("v_h", [SL, D], F32).ap()
    g1_h = nc.dram_tensor("g1_h", [NT, 128, NJ * 128], F32).ap()
    g2_h = nc.dram_tensor("g2_h", [NT, 128, D], F32).ap()
    summ_t = nc.dram_tensor("summ", [3 * 128, NJ * 64], F32)
    summ_all_t = nc.dram_tensor("summ_all", [CPB * 3 * 128, NJ * 64], F32)

    S = Sched()
    stack = ExitStack()
    with stack:
        sbuf = lambda name, shape, dt=F32: stack.enter_context(nc.sbuf_tensor(name, shape, dt))

        ident = sbuf("identt", [128, 128])
        maskL = sbuf("maskl", [128, 128])
        pm = sbuf("pms", [128, CPB])
        wscv = sbuf("wscv", [128, 1])
        bias = sbuf("biasr", [128, D])
        qint = sbuf("qint", [128, max(2 * NPW, 2 * D)], I32)
        enc = qint[:, 0:NPW]
        pk = qint[:, NPW:2 * NPW]
        q32b = qint[:, D:2 * D]
        pv = sbuf("pvs", [128, 1])
        p7 = sbuf("p7s", [128, 1])
        p7i = sbuf("p7i", [128, 1], I32)
        mrs = sbuf("mrs", [128, 1])
        halfv = sbuf("halfv", [128, 1])
        tffn = sbuf("tffn", [128, KC * D], BF16)
        qxtb = sbuf("qxtb", [128, D], BF16)
        WW = NJ * 64
        lnxT = sbuf("lnxT", [128, max(KC * SL, 8 * D + 4 * CPB * WW)])
        wun = lnxT[:, 0:R * D]
        wbig = sbuf("wbig", [128, max(KC * D, 2 * NJ * 128 + 4 * D)])
        ktc2 = [wbig[:, 0:NJ * 128], wbig[:, NJ * 128:2 * NJ * 128]]
        ktok2 = [wbig[:, 2 * NJ * 128:2 * NJ * 128 + D],
                 wbig[:, 2 * NJ * 128 + D:2 * NJ * 128 + 2 * D]]
        qtok2 = [wbig[:, 2 * NJ * 128 + 2 * D:2 * NJ * 128 + 3 * D],
                 wbig[:, 2 * NJ * 128 + 3 * D:2 * NJ * 128 + 4 * D]]
        xt = [wbig[:, 0:D], wbig[:, D:2 * D]]
        stgbig = sbuf("stgbig", [128, 1024])
        stg = [stgbig[:, 0:512], stgbig[:, 512:1024]]
        qtc2 = [sbuf("qtc0", [128, NJ * 128]), sbuf("qtc1", [128, NJ * 128])]
        vc2 = [sbuf("vc0", [128, D]), sbuf("vc1", [128, D])]
        g1c2 = [sbuf("g1c0", [128, NJ * 128]), sbuf("g1c1", [128, NJ * 128])]
        g2c2 = [sbuf("g2c0", [128, D]), sbuf("g2c1", [128, D])]
        otc3 = [sbuf("otc0", [128, NJ * 128]), sbuf("otc1", [128, NJ * 128]),
                sbuf("otc2x", [128, NJ * 128])]
        asm = [sbuf("asm0", [128, 128]), sbuf("asm1", [128, 128]),
               sbuf("asm2", [128, 128]), sbuf("asm3", [128, 128])]
        g1f2 = [sbuf("g1f0", [128, 128]), sbuf("g1f1", [128, 128])]
        g2f2 = [sbuf("g2f0", [128, 128]), sbuf("g2f1", [128, 128])]
        s1run = sbuf("s1run", [128, NJ * 128])
        hloc = sbuf("hloc", [128, NJ * 64])
        crun = sbuf("crun", [128, NJ * 64])
        s1in = sbuf("s1in", [128, NJ * 64])
        hin = sbuf("hin", [128, NJ * 64])
        hbd = sbuf("hbd", [128, NJ * 128])
        s1bd = sbuf("s1bd", [128, NJ * 128])
        pbd = otc3[0][:, 0:NJ * 128]
        tmpst = otc3[1][:, 0:NJ * 64]
        nmv = sbuf("nmv", [128, 1])
        vsv = sbuf("vsv", [128, 1])
        rrv = sbuf("rrv", [128, 1])
        rsv = sbuf("rsv", [128, 1])
        nwa = sbuf("nwa", [128, 1])
        sqd = stgbig[:, 0:min(D, 1024)]
        mxv = sbuf("mxv", [128, 1])
        scv = sbuf("scv", [128, 1])
        epv = sbuf("epv", [128, 1])
        q32 = qint[:, 0:D]
        sa_a = lnxT[:, 7 * D + 0 * CPB * WW: 7 * D + 1 * CPB * WW]
        sa_b = lnxT[:, 7 * D + 1 * CPB * WW: 7 * D + 2 * CPB * WW]
        sa_c = lnxT[:, 7 * D + 2 * CPB * WW: 7 * D + 3 * CPB * WW]
        pfx = lnxT[:, 7 * D + 3 * CPB * WW: 7 * D + 4 * CPB * WW]
        x1 = lnxT[:, 0 * D:1 * D]
        xr_extra = lnxT[:, 7 * D + 4 * CPB * WW: 8 * D + 4 * CPB * WW]
        ln2 = lnxT[:, 1 * D:2 * D]
        abv = lnxT[:, 2 * D:3 * D]
        sgv = lnxT[:, 3 * D:4 * D]
        out2 = lnxT[:, 5 * D:6 * D]
        xr3 = [lnxT[:, 6 * D:7 * D], lnxT[:, 4 * D:5 * D], xr_extra]

        pb = [stack.enter_context(nc.psum_tensor(f"pb{i}", [128, 512], F32)) for i in range(8)]
        bank_readers = [[] for _ in range(8)]
        bank_writer = [None] * 8

        NSLOT = 8
        ld_slots = [f"ld{i}" for i in range(NSLOT)]
        lc_slots = [f"lc{i}" for i in range(4)]
        sp_slots = [f"sp{i}" for i in range(NSLOT)]
        S.setup(nc, stack, ld_slots + lc_slots + sp_slots + ["pe", "dv", "ac", "gp", "cc"])
        dma_idx = {"sync": 0, "gpsimd": 0, "scalar": 0}

        def _dma(eng, slots, dst, src, waits):
            i = dma_idx[eng]
            dma_idx[eng] += 1
            ch = slots[i % len(slots)]
            prev = S.cnt[ch]
            return S.run(eng, lambda e, d=dst, s=src: e.dma_start(out=d, in_=s),
                         list(waits) + [(ch, prev)], (ch, 16))

        def dma_in(dst, src, waits=()):
            return _dma("sync", ld_slots, dst, src, waits)

        def dma_in2(dst, src, waits=()):
            if cfg.get("no_scalar_dma"):
                return _dma("sync", ld_slots, dst, src, waits)
            return _dma("scalar", lc_slots, dst, src, waits)

        def dma_out(dst, src, waits=()):
            return _dma("gpsimd", sp_slots, dst, src, waits)

        def mm(bi, o, l, r, start, stop, tp=None, waits=(), skip=False):
            w = list(waits) + bank_readers[bi]
            bank_readers[bi] = []
            t = S.run("tensor",
                      lambda e, o=o, l=l, r=r, a=start, z=stop, t=tp, k=skip:
                      e.matmul(o, l, r, start=a, stop=z, tile_position=t, skip_group_check=k),
                      w, ("pe", 1))
            bank_writer[bi] = t
            return t

        def tr(bi, o, i, waits=()):
            w = list(waits) + bank_readers[bi]
            bank_readers[bi] = []
            t = S.run("tensor", lambda e, o=o, i=i: e.transpose(o, i, ident[0:i.shape[0], 0:i.shape[0]]),
                      w, ("pe", 1))
            bank_writer[bi] = t
            return t

        def vop(fn, waits=(), reads_bank=None):
            if reads_bank is not None:
                waits = list(waits) + [bank_writer[reads_bank]]
            t = S.run("vector", fn, waits, ("dv", 1))
            if reads_bank is not None:
                bank_readers[reads_bank].append(t)
            return t

        def aop(fn, waits=(), reads_bank=None):
            if reads_bank is not None:
                waits = list(waits) + [bank_writer[reads_bank]]
            t = S.run("scalar", fn, waits, ("ac", 1))
            if reads_bank is not None:
                bank_readers[reads_bank].append(t)
            return t

        def evac(dst, src, bi, waits=(), use="v"):
            if use == "v":
                return vop(lambda e, d=dst, s=src: e.tensor_copy(d, s), waits, reads_bank=bi)
            return aop(lambda e, d=dst, s=src: e.copy(d, s), waits, reads_bank=bi)

        def tr_slot(k):
            bk = k % 2
            bi = 7 if bk else 0
            sl = pb[bi][:, ((k // 2) % 2) * 128:(((k // 2) % 2) + 1) * 128]
            return bi, sl

        t_id = dma_in(ident[:, :], ident_in)
        t_p7 = vop(lambda e: e.memset(p7[:, :], 0.7))
        t_half = vop(lambda e: e.memset(halfv[:, :], 0.5))
        t_p7b = aop(lambda e: e.activation(p7i[:, :], p7[:, :], AF.Identity), [t_p7])
        t_pvp = aop(lambda e: e.activation(pv[:, :], p7i[:, :], AF.Identity), [t_p7b])
        t_mk = dma_in(maskL[:, :], maskL_in)
        t_pm = dma_in(pm[:, :], pm_in)
        t_wsc = dma_in(wscv[:, :], wsc_in)
        t_bias = dma_in(bias[:, :], bias_in)
        t_wf = []
        t_pk_prev = None
        NWCH = R * D // NPW
        for cw in range(NWCH):
            t_pk = dma_in(pk[:, :], pk_in[:, cw * NPW:(cw + 1) * NPW], [t_pk_prev])
            t_es = []
            for f in range(4):
                t_es.append(vop(lambda e, f=f: e.tensor_scalar(enc[32 * f:32 * (f + 1), :],
                                                               pk[32 * f:32 * (f + 1), :], 2 * f, 3,
                                                               ALU.logical_shift_right, ALU.bitwise_and),
                                [t_pk]))
            t_wf.append(vop(lambda e, cw=cw: e.tensor_scalar(wun[:, cw * NPW:(cw + 1) * NPW], enc[:, :],
                                                             1.0, None, ALU.subtract), t_es))
            t_pk_prev = t_wf[-1]
        tffn_toks = []
        si = 0
        for r in range(R):
            for ic in range(KC):
                bi, sl = tr_slot(si)
                t_t = tr(bi, sl, wun[:, r * D + ic * 128: r * D + ic * 128 + 128], [t_id] + t_wf)
                dst = tffn[:, ic * D + r: ic * D + r + R * 127 + 1: R]
                tffn_toks.append(evac(dst, sl, bi, [t_t], use=("v" if si % 2 else "a")))
                si += 1
        t_tffn = [max([t for t in tffn_toks if t[0] == "dv"], key=lambda x: x[1]),
                  max([t for t in tffn_toks if t[0] == "ac"], key=lambda x: x[1])]
        if cfg.get("debug7"):
            dbg7 = nc.dram_tensor("dbg7", [128, 2 * D], BF16, kind="ExternalOutput").ap()
            dma_out(dbg7[:, :], tffn[:, 0:2 * D], t_tffn)

        tile_tok = [None] * NT
        xt_free = [[], []]
        for i in range(NT):
            bi_ = i % 2
            t_x = dma_in(xt[bi_][:, :], x_in[i * 128:(i + 1) * 128, :], xt_free[bi_])
            t_s1 = vop(lambda e, b=bi_: e.tensor_reduce(nmv[:, :], xt[b][:, :], AX.X, ALU.add),
                       [t_x, tile_tok[i - 1][0] if i else None])
            t_nm = vop(lambda e: e.tensor_scalar(nmv[:, :], nmv[:, :], -1.0 / D, None, ALU.mult), [t_s1])
            t_sq = aop(lambda e, b=bi_: e.activation(sqd[:, :], xt[b][:, :], AF.Square, bias=nmv[:, :], scale=1.0,
                                                     accum_out=vsv[:, :]), [t_nm])
            t_v1 = vop(lambda e: e.tensor_scalar(vsv[:, :], vsv[:, :], 1.0 / D, 1e-5, ALU.mult, ALU.add), [t_sq])
            t_rr = vop(lambda e: e.reciprocal(rrv[:, :], vsv[:, :]), [t_v1])
            t_rs = aop(lambda e: e.activation(rsv[:, :], rrv[:, :], AF.Sqrt), [t_rr])
            t_n1 = vop(lambda e: e.tensor_tensor(nwa[:, :], rsv[:, :], rsv[:, :], ALU.mult), [t_rs])
            t_n2 = vop(lambda e: e.tensor_tensor(nwa[:, :], nwa[:, :], vsv[:, :], ALU.mult), [t_n1])
            t_n3 = vop(lambda e: e.tensor_scalar(nwa[:, :], nwa[:, :], -0.5, 1.5, ALU.mult, ALU.add), [t_n2])
            t_n4 = vop(lambda e: e.tensor_tensor(rsv[:, :], rsv[:, :], nwa[:, :], ALU.mult), [t_n3])
            t_ln = vop(lambda e, b=bi_: e.tensor_scalar(xt[b][:, :], xt[b][:, :], nmv[:, :], rsv[:, :],
                                                        ALU.add, ALU.mult), [t_n4])
            toks = []
            for kc in range(KC):
                bi2, sl = tr_slot(kc)
                t_t = tr(bi2, sl, xt[bi_][:, kc * 128:(kc + 1) * 128], [t_ln, t_id])
                dst = lnxT[:, kc * SL + i * 128: kc * SL + (i + 1) * 128]
                toks.append(evac(dst, sl, bi2, [t_t], use=("v" if kc % 2 == 0 else "a")))
            dmax = max([t for t in toks if t[0] == "dv"], key=lambda x: x[1], default=None)
            amax = max([t for t in toks if t[0] == "ac"], key=lambda x: x[1], default=None)
            tile_tok[i] = (dmax, amax)
            xt_free[bi_] = [dmax, amax]

        def tiles_tok(i0, i1):
            o_ = []
            for i in range(i0, i1):
                o_ += [tile_tok[i][0], tile_tok[i][1]]
            return o_

        stg_free = [None, None]
        last_w_mm = None
        t_p1_done = tiles_tok(NT - 1, NT)
        for wsrc, dsth in ((wq_in, qt_h), (wk_in, kt_h)):
            t_w = []
            for kc in range(KC):
                t_w.append(dma_in(wbig[:, kc * D:(kc + 1) * D], wsrc[kc * 128:(kc + 1) * 128, :],
                                  [last_w_mm] + t_p1_done))
            for oc in range(NJ):
                for tg in range(SL // 512):
                    par = tg % 2
                    bi = 2 + par
                    t_m = None
                    for kc in range(KC):
                        t_m = mm(bi, pb[bi][:, :], wbig[:, kc * D + oc * 128: kc * D + oc * 128 + 128],
                                 lnxT[:, kc * SL + tg * 512: kc * SL + tg * 512 + 512],
                                 start=(kc == 0), stop=(kc == KC - 1),
                                 waits=[t_w[kc]] + tiles_tok(tg * 4, tg * 4 + 4))
                    last_w_mm = t_m
                    t_ev = evac(stg[par][:, :], pb[bi][:, :], bi, [t_m, stg_free[par]])
                    t_sp = dma_out(dsth[:, oc, tg * 512:(tg + 1) * 512], stg[par][:, :], [t_ev])
                    stg_free[par] = t_sp

        t_wv = []
        for kc in range(KC):
            t_wv.append(dma_in(wbig[:, kc * D:(kc + 1) * D], wv_in[kc * 128:(kc + 1) * 128, :], [last_w_mm]))
        for i in range(NT):
            for nh in range(NH):
                par = (i * NH + nh) % 2
                bi = 2 + par
                t_m = None
                for kc in range(KC):
                    t_m = mm(bi, pb[bi][:, :], lnxT[:, kc * SL + i * 128: kc * SL + (i + 1) * 128],
                             wbig[:, kc * D + nh * 512: kc * D + nh * 512 + 512],
                             start=(kc == 0), stop=(kc == KC - 1),
                             waits=[t_wv[kc]] + tiles_tok(i, i + 1))
                last_w_mm = t_m
                t_ev = evac(stg[par][:, :], pb[bi][:, :], bi, [t_m, stg_free[par]])
                t_sp = dma_out(v_h[i * 128:(i + 1) * 128, nh * 512:(nh + 1) * 512], stg[par][:, :], [t_ev])
                stg_free[par] = t_sp
        t_spills_l = [stg_free[0], stg_free[1]]

        SA = cfg.get("stop_after", 9)
        if SA < 3:
            S.run("gpsimd", None, waits=[(ch, S.cnt[ch]) for ch in sp_slots])
            with nc.Block() as block:
                S.emit_into(block)
            return nc
        t_z1 = vop(lambda e: e.memset(s1run[:, :], 0.0))
        t_z2 = vop(lambda e: e.memset(hloc[:, :], 0.0))
        t_z3 = vop(lambda e: e.memset(crun[:, :], 0.0))
        s1_tok = t_z1
        hl_tok = t_z2
        cr_tok = t_z3
        u3 = {"qtc": [[], []], "ktc": [[], []], "vc": [[], []],
              "ktok": [[], []], "qtok": [[], []]}
        loads3 = [None] * (NT + 1)

        def p3_load(c):
            p_ = c % 2
            ex = t_spills_l if c < 2 else []
            tq = dma_in(qtc2[p_][:, :], qt_h[:, :, c * 128:(c + 1) * 128], ex + u3["qtc"][p_])
            tk = dma_in(ktc2[p_][:, :], kt_h[:, :, c * 128:(c + 1) * 128], ex + u3["ktc"][p_])
            tv = dma_in2(vc2[p_][:, :], v_h[c * 128:(c + 1) * 128, :], ex + u3["vc"][p_])
            u3["qtc"][p_] = []; u3["ktc"][p_] = []; u3["vc"][p_] = []
            return (tq, tk, tv)

        loads3[0] = p3_load(0)
        g1_sp_par = [None, None]
        g2c_rd = [[], []]
        g1c_rd = [[], []]
        for c in range(NT):
            p = c % 2
            t_qt, t_kt, t_vl = loads3[c]
            if c + 1 < NT:
                loads3[c + 1] = p3_load(c + 1)
            qtc = qtc2[p]; ktc = ktc2[p]; vc = vc2[p]
            ktok = ktok2[p]; qtok = qtok2[p]
            g1c = g1c2[p]; g2c = g2c2[p]
            g1c_rd_old = g1c_rd[p]
            g1c_rd[p] = []
            g2c_rd_old = g2c_rd[p]
            g2c_rd[p] = []
            old_ktok_u = u3["ktok"][p]; u3["ktok"][p] = []
            old_qtok_u = u3["qtok"][p]; u3["qtok"][p] = []
            ktok_toks = [None] * NJ
            qtok_toks = [None] * NJ
            for jj in range(NJ):
                bi, sl = tr_slot(jj)
                t_t = tr(bi, sl, ktc[:, jj * 128:(jj + 1) * 128], [t_kt] + old_ktok_u)
                u3["ktc"][p].append(t_t)
                ktok_toks[jj] = evac(ktok[:, jj * 128:(jj + 1) * 128], sl, bi, [t_t],
                                     use=("v" if jj % 2 else "a"))
            for jj in range(NJ):
                bi, sl = tr_slot(jj + NJ)
                t_t = tr(bi, sl, qtc[:, jj * 128:(jj + 1) * 128], [t_qt] + old_qtok_u)
                u3["qtc"][p].append(t_t)
                qtok_toks[jj] = evac(qtok[:, jj * 128:(jj + 1) * 128], sl, bi, [t_t],
                                     use=("v" if jj % 2 else "a"))

            am_toks = [None] * NJ

            def stage_A(jj):
                jw = slice(jj * 128, (jj + 1) * 128)
                res = []
                for hh, b in ((0, 0), (1, 64)):
                    bs = slice(b, b + 64)
                    bi = 4 - hh
                    t_as = mm(bi, pb[bi][:, 0:128], ktc[bs, jw], qtc[bs, jw], True, True,
                              tp=(b, 0), waits=[t_kt, t_qt])
                    u3["qtc"][p].append(t_as); u3["ktc"][p].append(t_as)
                    ai = (jj % 2) * 2 + hh
                    res.append(vop(lambda e, a=ai, bb=bi: e.tensor_tensor(asm[a][:, :], pb[bb][:, 0:128],
                                                                         maskL[:, :], ALU.mult),
                                   [t_as, t_mk], reads_bank=bi))
                am_toks[jj] = res

            g1_evs, g2_evs = [], []
            corr_toks = []
            dS1_mms = []
            dH_mms = []

            def stage_G(jj):
                jw = slice(jj * 128, (jj + 1) * 128)
                jw64 = slice(jj * 64, (jj + 1) * 64)
                t_am = am_toks[jj]
                for hh, b in ((0, 0), (1, 64)):
                    bs = slice(b, b + 64)
                    hc = slice((2 * jj + hh) * 64, (2 * jj + hh) * 64 + 64)
                    ai = (jj % 2) * 2 + hh
                    t_a = mm(5, pb[5][bs, 0:128], ktok[:, hc], asm[ai][:, :], True, False, tp=(0, b),
                             waits=[ktok_toks[jj], t_am[hh]], skip=True)
                    u3["ktok"][p].append(t_a)
                t_b = mm(5, pb[5][:, 0:128], s1run[:, jw], qtc[:, jw], False, True, tp=(0, 0),
                         waits=[s1_tok], skip=True)
                u3["qtc"][p].append(t_b)
                corr_toks.append(t_b)
                t_g1ev = vop(lambda e, d=g1c[:, jw]: e.tensor_copy(d, pb[5][:, 0:128]),
                             [t_b, g1_sp_par[p]] + g1c_rd_old, reads_bank=5)
                g1_evs.append(t_g1ev)
                bi6, sl6 = tr_slot(2 * jj + (c % 2))
                t_t2 = tr(bi6, sl6, g1c[:, jw], [t_g1ev])
                g1c_rd[p].append(t_t2)
                t_g2ev = evac(g2c[:, jw], sl6, bi6, [t_t2] + g2c_rd_old, use="a")
                g2_evs.append(t_g2ev)
                for hh, b in ((0, 0), (1, 64)):
                    hc = slice((2 * jj + hh) * 64, (2 * jj + hh) * 64 + 64)
                    t_d = mm(1, pb[1][slice(b, b + 64), jw64], ktok[:, hc], ktok[:, hc], True, True,
                             tp=(0, b), waits=[ktok_toks[jj]])
                    u3["ktok"][p].append(t_d)
                    dS1_mms.append(t_d)
                for hh, b in ((0, 0), (1, 64)):
                    hc = slice((2 * jj + hh) * 64, (2 * jj + hh) * 64 + 64)
                    t_d = mm(2, pb[2][slice(b, b + 64), jw64], g2c[:, hc], vc[:, hc], True, True,
                             tp=(0, b), waits=[t_g2ev, t_vl])
                    u3["vc"][p].append(t_d)
                    g2c_rd[p].append(t_d)
                    dH_mms.append(t_d)

            stage_A(0)
            for jj in range(NJ):
                if jj + 1 < NJ:
                    stage_A(jj + 1)
                stage_G(jj)

            s1e_o = s1run[0:64, :].rearrange("p (j c) -> p j c", c=128)[:, :, 0:64]
            s1e_i = pb[1][0:64, 0:NJ * 64].rearrange("p (j c) -> p j c", c=64)
            s1o_o = s1run[64:128, :].rearrange("p (j c) -> p j c", c=128)[:, :, 64:128]
            s1o_i = pb[1][64:128, 0:NJ * 64].rearrange("p (j c) -> p j c", c=64)
            sa1a = vop(lambda e: e.tensor_tensor(s1e_o, s1e_i, s1e_o, ALU.add),
                       [s1_tok] + dS1_mms + corr_toks, reads_bank=1)
            sa1 = vop(lambda e: e.tensor_tensor(s1o_o, s1o_i, s1o_o, ALU.add),
                      [sa1a], reads_bank=1)
            s1_tok = sa1
            ha1 = vop(lambda e: e.tensor_tensor(hloc[:, :], pb[2][:, 0:NJ * 64], hloc[:, :], ALU.add),
                      [hl_tok] + dH_mms, reads_bank=2)
            hl_tok = ha1
            dC_mms = []
            for h in range(H):
                jj, hh = h // 2, h % 2
                hc = slice(h * 64, h * 64 + 64)
                t_d = mm(1, pb[1][slice(64 * hh, 64 * hh + 64), jj * 64:(jj + 1) * 64],
                         qtok[:, hc], vc[:, hc], True, True, tp=(0, 64 * hh),
                         waits=[qtok_toks[jj], t_vl])
                u3["qtok"][p].append(t_d)
                u3["vc"][p].append(t_d)
                dC_mms.append(t_d)
            ca1 = vop(lambda e: e.tensor_tensor(crun[:, :], pb[1][:, 0:NJ * 64], crun[:, :], ALU.add),
                      [cr_tok] + dC_mms, reads_bank=1)
            cr_tok = ca1
            g1_sp_par[p] = dma_out(g1_h[c, :, :], g1c[:, :], g1_evs)

        if SA < 4:
            S.run("gpsimd", None, waits=[(ch, S.cnt[ch]) for ch in sp_slots])
            with nc.Block() as block:
                S.emit_into(block)
            return nc
        u5 = {"qtc": [[], []], "vc": [[], []], "g1": [[], []], "g2": [[], []], "otc": [[], [], []]}
        u5["qtc"][0] = [t for t in u3["qtc"][0] if t is not None]
        u5["qtc"][1] = [t for t in u3["qtc"][1] if t is not None]
        u5["vc"][0] = u3["vc"][0]; u5["vc"][1] = u3["vc"][1]
        u5["g1"][0] = [g1_sp_par[0]] + g1c_rd[0]
        u5["g1"][1] = [g1_sp_par[1]] + g1c_rd[1]

        loads5 = [None] * (NT + 1)

        def p5_load(c):
            p_ = c % 2
            tq = dma_in(qtc2[p_][:, :], qt_h[:, :, c * 128:(c + 1) * 128], u5["qtc"][p_])
            tv = dma_in2(vc2[p_][:, :], v_h[c * 128:(c + 1) * 128, :], u5["vc"][p_])
            tg1 = dma_in(g1c2[p_][:, :], g1_h[c, :, :], u5["g1"][p_])
            u5["qtc"][p_] = []; u5["vc"][p_] = []; u5["g1"][p_] = []
            return (tq, tv, tg1, None)

        loads5[0] = p5_load(0)
        if NT > 1:
            loads5[1] = p5_load(1)

        tp_e = tmpst[0:64, :].rearrange("p (j c) -> p j c", c=64)
        tp_o = tmpst[64:128, :].rearrange("p (j c) -> p j c", c=64)
        s1bde = s1run[0:64, :].rearrange("p (j c) -> p j c", c=128)[:, :, 0:64]
        s1bdo = s1run[64:128, :].rearrange("p (j c) -> p j c", c=128)[:, :, 64:128]
        t_x1e = vop(lambda e: e.tensor_copy(tp_e, s1bde), [s1_tok])
        t_x1o = vop(lambda e: e.tensor_copy(tp_o, s1bdo), [t_x1e])
        t_sm0 = dma_out(summ_t.ap()[0:128, :], tmpst[:, :], [t_x1o])
        t_sm1 = dma_out(summ_t.ap()[128:256, :], hloc[:, :], [hl_tok])
        t_sm2 = dma_out(summ_t.ap()[256:384, :], crun[:, :], [cr_tok])
        groups = [list(range(g * CPB, (g + 1) * CPB)) for g in range(B)]
        t_cc = S.run("gpsimd",
                     lambda e: e.collective_compute("AllGather", ALU.bypass, replica_groups=groups,
                                                    ins=[summ_t.ap().opt()], outs=[summ_all_t.ap().opt()]),
                     [t_sm0, t_sm1, t_sm2], ("cc", 1))
        sall3 = summ_all_t.ap().rearrange("(i k p) w -> i k p w", k=3, p=128)
        t_la = []
        for kind, dst in ((0, sa_a), (1, sa_b), (2, sa_c)):
            t_la.append(dma_in(dst.rearrange("p (i w) -> p i w", i=CPB),
                               sall3[:, kind, :, :].rearrange("i p w -> p i w"), [t_cc]))
        t_pz = vop(lambda e: e.memset(pfx[:, 0:WW], 0.0))
        t_pv = [t_pz]
        for i in range(1, CPB):
            t_pv.append(vop(lambda e, i=i: e.tensor_tensor(pfx[:, i * WW:(i + 1) * WW], pfx[:, (i - 1) * WW:i * WW],
                                                           sa_a[:, (i - 1) * WW:i * WW], ALU.add), [t_pv[-1]] + t_la))
        t_s1z = vop(lambda e: e.memset(s1in[:, :], 0.0))
        t_acc = t_s1z
        for i in range(CPB):
            t_t1 = vop(lambda e, i=i: e.tensor_scalar(tmpst[:, :], sa_a[:, i * WW:(i + 1) * WW], pm[:, i:i + 1], None,
                                                      ALU.mult), t_la + [t_pm, t_acc, t_sm0])
            t_acc = vop(lambda e: e.tensor_tensor(s1in[:, :], s1in[:, :], tmpst[:, :], ALU.add), [t_t1, t_acc])
        t_s1in = t_acc
        t_bz = vop(lambda e: e.memset(s1bd[:, :], 0.0))
        t_bd = [t_bz]
        for jj in range(NJ):
            t_bd.append(vop(lambda e, jj=jj: e.tensor_copy(s1bd[0:64, jj * 128:jj * 128 + 64],
                                                           s1in[0:64, jj * 64:(jj + 1) * 64]), [t_s1in, t_bz]))
            t_bd.append(vop(lambda e, jj=jj: e.tensor_copy(s1bd[64:128, jj * 128 + 64:jj * 128 + 128],
                                                           s1in[64:128, jj * 64:(jj + 1) * 64]), [t_s1in, t_bz]))
        t_s1bd = [t_bd[-1]]
        t_hz = vop(lambda e: e.memset(hin[:, :], 0.0))
        t_hacc = t_hz
        prev_pbd_users = []
        for i in range(CPB):
            if i == 0:
                t_t1 = vop(lambda e: e.tensor_scalar(tmpst[:, :], sa_b[:, 0:WW], pm[:, 0:1], None, ALU.mult),
                           t_la + [t_pm, t_acc, t_hacc])
                t_hacc = vop(lambda e: e.tensor_tensor(hin[:, :], hin[:, :], tmpst[:, :], ALU.add), [t_t1, t_hacc])
                continue
            t_pbz = vop(lambda e: e.memset(pbd[:, :], 0.0), prev_pbd_users)
            t_pb = [t_pbz]
            for jj in range(NJ):
                t_pb.append(vop(lambda e, jj=jj, i=i: e.tensor_copy(pbd[0:64, jj * 128:jj * 128 + 64],
                                                                   pfx[0:64, i * WW + jj * 64: i * WW + (jj + 1) * 64]),
                                [t_pv[i], t_pbz]))
                t_pb.append(vop(lambda e, jj=jj, i=i: e.tensor_copy(pbd[64:128, jj * 128 + 64:jj * 128 + 128],
                                                                   pfx[64:128, i * WW + jj * 64: i * WW + (jj + 1) * 64]),
                                [t_pv[i], t_pbz]))
            t_x = None
            prev_pbd_users = []
            for jj in range(NJ):
                t_x = mm(2, pb[2][:, jj * 64:(jj + 1) * 64], pbd[:, jj * 128:(jj + 1) * 128],
                         sa_c[:, i * WW + jj * 64: i * WW + (jj + 1) * 64], True, True,
                         waits=t_pb + t_la)
                prev_pbd_users.append(t_x)
            t_t1 = vop(lambda e, i=i: e.tensor_tensor(tmpst[:, :], sa_b[:, i * WW:(i + 1) * WW], pb[2][:, 0:WW], ALU.add),
                       [t_x, t_hacc], reads_bank=2)
            t_t2 = vop(lambda e, i=i: e.tensor_scalar(tmpst[:, :], tmpst[:, :], pm[:, i:i + 1], None, ALU.mult), [t_t1])
            t_hacc = vop(lambda e: e.tensor_tensor(hin[:, :], hin[:, :], tmpst[:, :], ALU.add), [t_t2])
        t_hbz = vop(lambda e: e.memset(hbd[:, :], 0.0), [t_hacc] if False else [])
        t_hb = [t_hbz]
        for jj in range(NJ):
            t_hb.append(vop(lambda e, jj=jj: e.tensor_copy(hbd[0:64, jj * 128:jj * 128 + 64],
                                                           hin[0:64, jj * 64:(jj + 1) * 64]), [t_hacc, t_hbz]))
            t_hb.append(vop(lambda e, jj=jj: e.tensor_copy(hbd[64:128, jj * 128 + 64:jj * 128 + 128],
                                                           hin[64:128, jj * 64:(jj + 1) * 64]), [t_hacc, t_hbz]))
        t_hin = t_hb[-1]

        if cfg.get("debug"):
            dbg = nc.dram_tensor("dbg", [5 * 128, NJ * 64], F32, kind="ExternalOutput").ap()
            dma_out(dbg[0 * 128:1 * 128, :], s1run[:, :], [s1_tok])
            dma_out(dbg[1 * 128:2 * 128, :], hloc[:, :], [hl_tok])
            dma_out(dbg[2 * 128:3 * 128, :], crun[:, :], [cr_tok])
            dma_out(dbg[3 * 128:4 * 128, :], s1in[:, :], [t_s1in])
            t_hin = dma_out(dbg[4 * 128:5 * 128, :], hin[:, :], [t_hin])

        t_wo = []
        for jj in range(NJ):
            t_wo.append(dma_in(wbig[:, jj * D:(jj + 1) * D], wo_in[jj * 128:(jj + 1) * 128, :],
                               [last_w_mm, cr_tok, s1_tok, hl_tok]))

        if SA < 5:
            S.run("gpsimd", None, waits=[(ch, S.cnt[ch]) for ch in sp_slots])
            with nc.Block() as block:
                S.emit_into(block)
            return nc
        hr_tok = t_hin
        t_xr_tok = [None] * NT
        xr_free = [list(t_spills_l), list(t_spills_l), list(t_spills_l)]
        out_free = [None] * NH
        P5N = cfg.get("p5_chunks", NT)
        epi_stageq = []

        def epi_step():
            if epi_stageq:
                epi_stageq.pop(0)()
        for c in range(NT + 1):
            if c < P5N:
                p = c % 2
                t_qt, t_vl, t_g1, t_g2 = loads5[c]
                if c + 1 < P5N and loads5[c + 1] is None:
                    loads5[c + 1] = p5_load(c + 1)
                qtc = qtc2[p]; vc = vc2[p]; g1c = g1c2[p]; g2c = g2c2[p]
                p3_ = c % 3
                otc = otc3[p3_]
                t_xr = dma_in2(xr3[p3_], x_in[c * 128:(c + 1) * 128, :], xr_free[p3_])
                t_xr_tok[c] = t_xr

                gf_toks = [None] * NJ
                ot_evs = []
                dH_mms5 = []
                m6b_toks = []

                def stage_MA(jj):
                    jw = slice(jj * 128, (jj + 1) * 128)
                    bi = 2 + (jj % 2)
                    t_qs = mm(bi, pb[bi][:, 0:128], s1bd[:, jw], qtc[:, jw], True, True,
                              waits=t_s1bd + [t_qt])
                    u5["qtc"][p].append(t_qs)
                    gp = jj % 2
                    t_g1f = vop(lambda e, w=jw, g=gp, bb=bi, src=g1c: e.tensor_tensor(g1f2[g][:, :], pb[bb][:, 0:128],
                                                                            src[:, w], ALU.add),
                                [t_qs, t_g1] + ([gf_toks[jj - 2][2]] if jj >= 2 else []), reads_bank=bi)
                    u5["g1"][p].append(t_g1f)
                    bi7, sl7 = tr_slot(jj)
                    t_tg = tr(bi7, sl7, g1f2[gp][:, :], [t_g1f])
                    t_g2f = evac(g2f2[gp][:, :], sl7, bi7,
                                 [t_tg] + ([gf_toks[jj - 2][3]] if jj >= 2 else []), use="a")
                    gf_toks[jj] = [t_g1f, t_g2f, None, None]

                def stage_MB(jj):
                    jw = slice(jj * 128, (jj + 1) * 128)
                    jw64 = slice(jj * 64, (jj + 1) * 64)
                    gp = jj % 2
                    t_g1f, t_g2f = gf_toks[jj][0], gf_toks[jj][1]
                    t_bm = [None, None]
                    for hh, b in ((0, 0), (1, 64)):
                        bs = slice(b, b + 64)
                        bsb = 4 + hh
                        t_bs = mm(bsb, pb[bsb][:, 0:128], g1f2[gp][bs, :], qtc[bs, jw], True, True,
                                  tp=(b, 0), waits=[t_g1f])
                        u5["qtc"][p].append(t_bs)
                        ai = (jj % 2) * 2 + hh
                        t_bm[hh] = vop(lambda e, a=ai, bb=bsb: e.tensor_tensor(asm[a][:, :], pb[bb][:, 0:128],
                                                                              maskL[:, :], ALU.mult),
                                       [t_bs, t_mk], reads_bank=bsb)
                    t_hb6 = mm(6, pb[6][:, 0:128], hbd[:, jw], qtc[:, jw], True, False, tp=(0, 0),
                               waits=[hr_tok], skip=True)
                    u5["qtc"][p].append(t_hb6)
                    m6b_toks.append(t_hb6)
                    for hh, b in ((0, 0), (1, 64)):
                        hc = slice((2 * jj + hh) * 64, (2 * jj + hh) * 64 + 64)
                        t_d = mm(1, pb[1][slice(b, b + 64), jw64], g2f2[gp][:, hh * 64:hh * 64 + 64],
                                 vc[:, hc], True, True, tp=(0, b), waits=[t_g2f, t_vl])
                        u5["vc"][p].append(t_d)
                        dH_mms5.append(t_d)
                        gf_toks[jj][3] = t_d
                    t_last = None
                    for hh, b in ((0, 0), (1, 64)):
                        bs = slice(b, b + 64)
                        hc = slice((2 * jj + hh) * 64, (2 * jj + hh) * 64 + 64)
                        ai = (jj % 2) * 2 + hh
                        t_last = mm(6, pb[6][bs, 0:128], vc[:, hc], asm[ai][:, :], False, hh == 1,
                                    tp=(0, b), waits=[t_vl, t_bm[hh]], skip=True)
                        u5["vc"][p].append(t_last)
                    t_oe = evac(otc[:, jw], pb[6][:, 0:128], 6,
                                [t_last] + u5["otc"][p3_], use=("v" if jj % 2 else "a"))
                    if jj == 0:
                        u5["otc"][p3_] = []
                    gf_toks[jj][2] = t_oe
                    ot_evs.append(t_oe)

                if cfg.get("p5_only_loads"):
                    epi_pending = None
                    continue
                stage_MA(0)
                if NJ > 1:
                    stage_MA(1)
                epi_step()
                epi_step()
                if cfg.get("p5_ma_only"):
                    for jj in range(2, NJ):
                        stage_MA(jj)
                    epi_pending = None
                    continue
                for jj in range(NJ):
                    stage_MB(jj)
                    if jj + 2 < NJ:
                        stage_MA(jj + 2)
                    epi_step()
                hb_e = hbd[0:64, :].rearrange("p (j c) -> p j c", c=128)[:, :, 0:64]
                hb_ei = pb[1][0:64, 0:NJ * 64].rearrange("p (j c) -> p j c", c=64)
                hb_o = hbd[64:128, :].rearrange("p (j c) -> p j c", c=128)[:, :, 64:128]
                hb_oi = pb[1][64:128, 0:NJ * 64].rearrange("p (j c) -> p j c", c=64)
                hr_a = vop(lambda e: e.tensor_tensor(hb_e, hb_ei, hb_e, ALU.add),
                           [hr_tok] + dH_mms5 + m6b_toks, reads_bank=1)
                hr_tok = vop(lambda e: e.tensor_tensor(hb_o, hb_oi, hb_o, ALU.add),
                             [hr_a], reads_bank=1)
                epi_step()
                epi_step()

                def make_epi(c=c, p=p3_, ot_evs=tuple(ot_evs), t_xr=t_xr):
                    otc_ = otc3[p]
                    xr_ = xr3[p]
                    st = {}

                    def wo_nh(nh):
                        def f():
                            t_m = None
                            for jj in range(NJ):
                                t_m = mm(7, pb[7][:, :], otc_[:, jj * 128:(jj + 1) * 128],
                                         wbig[:, jj * D + nh * 512: jj * D + nh * 512 + 512],
                                         start=(jj == 0), stop=(jj == NJ - 1),
                                         waits=[t_wo[jj]] + list(ot_evs))
                                u5["otc"][p].append(t_m)
                            t_x1 = vop(lambda e, n=nh, xv=xr_: e.tensor_tensor(x1[:, n * 512:(n + 1) * 512], pb[7][:, :],
                                                                       xv[:, n * 512:(n + 1) * 512], ALU.add),
                                       [t_m, t_xr, out_free[nh]], reads_bank=7)
                            st.setdefault("x1ev", []).append(t_x1)
                        return f

                    def lnq_p1():
                        x1ev = st["x1ev"]
                        t_s1 = aop(lambda e: e.activation(sqd[:, :], x1, AF.Identity, accum_out=nmv[:, :]), x1ev)
                        st["nm"] = vop(lambda e: e.tensor_scalar(nmv[:, :], nmv[:, :], -1.0 / D, None, ALU.mult), [t_s1])

                    def lnq_p2():
                        t_sq = aop(lambda e: e.activation(sqd[:, :], x1, AF.Square, bias=nmv[:, :], scale=1.0,
                                                          accum_out=vsv[:, :]), [st["nm"]])
                        t_v1 = vop(lambda e: e.tensor_scalar(vsv[:, :], vsv[:, :], 1.0 / D, 1e-5, ALU.mult, ALU.add), [t_sq])
                        t_rr = vop(lambda e: e.reciprocal(rrv[:, :], vsv[:, :]), [t_v1])
                        st["rs"] = aop(lambda e: e.activation(rsv[:, :], rrv[:, :], AF.Sqrt), [t_rr])

                    def lnq_p3():
                        t_n1 = vop(lambda e: e.tensor_tensor(nwa[:, :], rsv[:, :], rsv[:, :], ALU.mult), [st["rs"]])
                        t_n2 = vop(lambda e: e.tensor_tensor(nwa[:, :], nwa[:, :], vsv[:, :], ALU.mult), [t_n1])
                        t_n3 = vop(lambda e: e.tensor_scalar(nwa[:, :], nwa[:, :], -0.5, 1.5, ALU.mult, ALU.add), [t_n2])
                        t_n4 = vop(lambda e: e.tensor_tensor(rsv[:, :], rsv[:, :], nwa[:, :], ALU.mult), [t_n3])
                        st["ms"] = vop(lambda e: e.tensor_tensor(mrs[:, :], nmv[:, :], rsv[:, :], ALU.mult), [t_n4])

                    def lnq_p4():
                        t_ln = aop(lambda e: e.activation(ln2, x1, AF.Identity, scale=rsv[:, :], bias=mrs[:, :]), [st["ms"]])
                        st["mx"] = vop(lambda e: e.tensor_reduce(mxv[:, :], ln2, AX.X, ALU.max, apply_absolute_value=True), [t_ln])

                    def lnq_p5():
                        t_mx2 = vop(lambda e: e.tensor_scalar(mxv[:, :], mxv[:, :], 1e-5, None, ALU.max), [st["mx"]])
                        t_sc = vop(lambda e: e.reciprocal(scv[:, :], mxv[:, :]), [t_mx2])
                        t_sc2 = vop(lambda e: e.tensor_scalar(scv[:, :], scv[:, :], 127.0, None, ALU.mult), [t_sc])
                        st["y"] = aop(lambda e: e.activation(abv, ln2, AF.Identity, scale=scv[:, :]), [t_sc2])
                        t_e1 = vop(lambda e: e.tensor_tensor(epv[:, :], mxv[:, :], wscv[:, :], ALU.mult), [t_mx2, t_wsc])
                        st["e2"] = vop(lambda e: e.tensor_scalar(epv[:, :], epv[:, :], 1.0 / 127.0, None, ALU.mult), [t_e1])

                    def lnq_p6():
                        t_r1 = aop(lambda e: e.activation(q32[:, :], abv, AF.Identity), [st["y"]])
                        st["r2"] = aop(lambda e: e.activation(q32b[:, :], abv, AF.Identity, bias=halfv[:, :]),
                                       [t_r1, t_half])

                    def lnq_p7():
                        t_b1 = vop(lambda e: e.tensor_tensor(sgv, q32[:, :], q32b[:, :], ALU.subtract), [st["r2"]])
                        t_b2 = vop(lambda e: e.tensor_scalar(sgv, sgv, pv[:, :], None, ALU.mult), [t_b1, t_pvp])
                        st["qf"] = vop(lambda e: e.tensor_tensor(abv, q32b[:, :], sgv, ALU.add), [t_b2])

                    def lnq_p8():
                        tq_toks = []
                        for kc in range(KC):
                            sl = pb[0][:, (kc % 4) * 128:((kc % 4) + 1) * 128]
                            t_t = tr(0, sl, abv[:, kc * 128:(kc + 1) * 128], [st["qf"]])
                            tq_toks.append(evac(qxtb[:, kc * 128:(kc + 1) * 128], sl, 0, [t_t],
                                                use=("v" if kc % 2 else "a")))
                        st["tq"] = tq_toks

                    def ffn_nh(nh):
                        def f():
                            t_m = None
                            for kc in range(KC):
                                t_m = mm(7, pb[7][:, :], qxtb[:, kc * 128:(kc + 1) * 128],
                                         tffn[:, kc * D + nh * 512: kc * D + nh * 512 + 512],
                                         start=(kc == 0), stop=(kc == KC - 1),
                                         waits=t_tffn + st["tq"])
                            t_f1 = vop(lambda e, n=nh: e.tensor_scalar(out2[:, n * 512:(n + 1) * 512], pb[7][:, :],
                                                                       epv[:, :], None, ALU.mult),
                                       [t_m, st["e2"], out_free[nh]], reads_bank=7)
                            t_f2 = vop(lambda e, n=nh: e.tensor_tensor(out2[:, n * 512:(n + 1) * 512],
                                                                       out2[:, n * 512:(n + 1) * 512],
                                                                       bias[:, n * 512:(n + 1) * 512], ALU.add),
                                       [t_f1, t_bias])
                            t_f3 = vop(lambda e, n=nh: e.tensor_tensor(out2[:, n * 512:(n + 1) * 512],
                                                                       out2[:, n * 512:(n + 1) * 512],
                                                                       x1[:, n * 512:(n + 1) * 512], ALU.add), [t_f2])
                            t_st = dma_out(out[c * 128:(c + 1) * 128, nh * 512:(nh + 1) * 512],
                                           out2[:, nh * 512:(nh + 1) * 512], [t_f3])
                            out_free[nh] = t_st
                            st.setdefault("sts", []).append(t_st)
                            if nh == NH - 1:
                                xr_free[p] = [t_st]
                            if cfg.get("debug5") and c == 0 and nh == NH - 1:
                                dbg5 = nc.dram_tensor("dbg5", [3 * 128, D], F32, kind="ExternalOutput").ap()
                                dma_out(dbg5[0:128, :], x1, [t_st])
                                dma_out(dbg5[128:256, :], out2[:, :], [t_st])
                                t_o5 = vop(lambda e: e.tensor_copy(sgv, qxtb[:, :]), [t_st])
                                dma_out(dbg5[256:384, :], sgv, [t_o5])
                                dbg6 = nc.dram_tensor("dbg6", [128, 2 * D], F32, kind="ExternalOutput").ap()
                                t_o6 = vop(lambda e: e.tensor_copy(x1, tffn[:, 0:D]), [t_st, t_o5])
                                dma_out(dbg6[:, 0:D], x1, [t_o6])
                                t_o7 = vop(lambda e: e.tensor_copy(x1, tffn[:, D:2 * D]), [t_o6])
                                dma_out(dbg6[:, D:2 * D], x1, [t_o7])
                        return f

                    stages = [wo_nh(0)]
                    for nh in range(1, NH):
                        stages.append(wo_nh(nh))
                    stages += [lnq_p1, lnq_p2, lnq_p3, lnq_p4, lnq_p5, lnq_p6, lnq_p7, lnq_p8]
                    for nh in range(NH):
                        stages.append(ffn_nh(nh))
                    return stages

                if not cfg.get("p5_no_epi"):
                    epi_stageq.extend(make_epi())
                if c == P5N - 1:
                    while epi_stageq:
                        epi_step()

        S.run("gpsimd", None, waits=[(ch, S.cnt[ch]) for ch in sp_slots])

        with nc.Block() as block:
            S.emit_into(block)

    return nc


def make_inputs(cfg, full):
    D = cfg["D"]; SL = cfg["SL"]; B = cfg["B"]; NCORE = cfg["NCORE"]
    CPB = NCORE // B
    x = np.ascontiguousarray(full["x"], dtype=np.float32)
    ident = np.eye(128, dtype=np.float32)
    maskL = (np.arange(128)[:, None] <= np.arange(128)[None, :]).astype(np.float32)
    bias_rep = np.ascontiguousarray(np.broadcast_to(full["bias"].astype(np.float32).reshape(1, D), (128, D)))
    wsc_rep = np.full((128, 1), np.float32(np.asarray(full["wscale"]).reshape(-1)[0]), dtype=np.float32)
    RD = D * D // 128
    pkw = np.asarray(full["packed"]).astype(np.int32).reshape(32, RD)
    pk = np.ascontiguousarray(pkw[np.arange(128) % 32])
    wq = np.ascontiguousarray(full["Wq"], dtype=np.float32)
    wk = np.ascontiguousarray(full["Wk"], dtype=np.float32)
    wv = np.ascontiguousarray(full["Wv"], dtype=np.float32)
    wo = np.ascontiguousarray(full["Wo"], dtype=np.float32)
    in_maps = []
    for c in range(NCORE):
        b, s = c // CPB, c % CPB
        pmask = np.zeros((128, CPB), dtype=np.float32)
        pmask[:, :s] = 1.0
        in_maps.append({
            "x_in": np.ascontiguousarray(x[b, s * SL:(s + 1) * SL, :]),
            "wq": wq, "wk": wk, "wv": wv, "wo": wo,
            "pk": pk, "wsc": wsc_rep, "bias_rep": bias_rep,
            "ident": ident, "maskL": maskL, "pmask": pmask,
        })
    return in_maps


def assemble_output(cfg, results):
    D = cfg["D"]; SL = cfg["SL"]; B = cfg["B"]; NCORE = cfg["NCORE"]
    CPB = NCORE // B
    out = np.empty((B, CPB * SL, D), dtype=np.float32)
    for c in range(NCORE):
        b, s = c // CPB, c % CPB
        out[b, s * SL:(s + 1) * SL, :] = results[c]["out"]
    return out








from concourse.bass_utils import run_bass_kernel_spmd

_CFG = dict(D=1024, SL=2048, B=2, NCORE=8)
_NC_CACHE = [None]


def kernel(L0_attn_x, L0_attn_W_q, L0_attn_W_k, L0_attn_W_v, L0_attn_W_o,
           L0_ffn_packed_weight, L0_ffn_weight_scale, L0_ffn_bias):
    cfg = _CFG
    full = {
        "x": np.asarray(L0_attn_x),
        "Wq": np.asarray(L0_attn_W_q), "Wk": np.asarray(L0_attn_W_k),
        "Wv": np.asarray(L0_attn_W_v), "Wo": np.asarray(L0_attn_W_o),
        "packed": np.asarray(L0_ffn_packed_weight),
        "wscale": np.asarray(L0_ffn_weight_scale),
        "bias": np.asarray(L0_ffn_bias),
    }
    if _NC_CACHE[0] is None:
        _NC_CACHE[0] = build_program(cfg)
    nc = _NC_CACHE[0]
    in_maps = make_inputs(cfg, full)
    res = run_bass_kernel_spmd(nc, in_maps, core_ids=list(range(cfg["NCORE"])))
    return assemble_output(cfg, res.results)
